# version 18
# baseline (speedup 1.0000x reference)
import numpy as np
import concourse.bass as bass
import concourse.mybir as mybir
from concourse.bass_utils import run_bass_kernel_spmd

F32 = mybir.dt.float32
BF16 = mybir.dt.bfloat16
AF = mybir.ActivationFunctionType
ALU = mybir.AluOpType
AX = mybir.AxisListType

D = 1024
S = 4096
NT = 32
OWN = 16
CP = 256
EPS = 1e-5
ENGINES = ("pe", "act", "dve", "sp")
SELF_SYNC = ("dve", "act")
LAST_SS = None

SLOT = {"ln1_0": 0, "ln1_1": 6, "kv_0": 12, "kv_1": 18, "cq_0": 24, "cq_1": 30,
        "attn_0": 36, "attn_1": 42, "ln2_0": 48, "ln2_1": 54, "ffn_0": 60,
        "ffn_1": 66, "attn_2": 72, "ln2_2": 78, "ffn_2": 84, "ln1_2": 90,
        "kv_2": 96}


class Sched:
    def __init__(self):
        self.ops = []

    def add(self, engine, fn, reads=(), writes=(), dbg="", cost=None):
        self.ops.append(dict(engine=engine, fn=fn, reads=tuple(reads),
                             writes=tuple(writes), dma=None, dbg=dbg,
                             cost=cost))

    def dma(self, queue, fn, reads=(), writes=(), sem=None, dbg="", cost=None):
        assert sem is not None
        self.ops.append(dict(engine=queue, fn=fn, reads=tuple(reads),
                             writes=tuple(writes), dma="dq_" + sem, dbg=dbg,
                             cost=cost))

    def barrier(self):
        self.ops.append(dict(engine=None, fn=None, reads=(), writes=(),
                             dma=None, dbg="barrier", barrier=True))

    def reorder(self):
        segs = []
        cur = []
        for op in self.ops:
            if op.get("barrier"):
                segs.append(cur)
                segs.append([op])
                cur = []
            else:
                cur.append(op)
        segs.append(cur)
        out = []
        for seg in segs:
            if len(seg) == 1 and seg[0].get("barrier"):
                out.extend(seg)
            else:
                out.extend(self._sched_seg(seg))
        self.ops = out

    def _sched_seg(self, seg):
        import heapq
        n = len(seg)
        if n <= 2:
            return seg
        preds = [set() for _ in range(n)]
        succs = [[] for _ in range(n)]
        led = {}
        lastq = {}
        for i, op in enumerate(seg):
            ps_r = tuple(k for k in op["reads"] if k.startswith("ps")
                         and k not in op["writes"])
            eff_w = op["writes"] + ps_r
            eff_r = tuple(k for k in op["reads"] if not k.startswith("ps"))
            dep = set()
            for r in eff_r:
                e = led.get(r)
                if e and e["w"] is not None:
                    dep.add(e["w"])
            for w in eff_w:
                e = led.get(w)
                if e:
                    if e["w"] is not None:
                        dep.add(e["w"])
                    dep.update(e["r"])
            if op["dma"]:
                q = op["dma"]
                if q in lastq:
                    dep.add(lastq[q])
                lastq[q] = i
            for w in eff_w:
                led[w] = {"w": i, "r": []}
            for r in eff_r:
                led.setdefault(r, {"w": None, "r": []})["r"].append(i)
            dep.discard(i)
            preds[i] = dep
            for d in dep:
                succs[d].append(i)
        cost = [op.get("cost") or (1500.0 if op["dma"] else 250.0)
                for op in seg]
        prio = [0.0] * n
        for i in range(n - 1, -1, -1):
            m = 0.0
            for s in succs[i]:
                if prio[s] > m:
                    m = prio[s]
            prio[i] = m + cost[i]
        npred = [len(p) for p in preds]
        engines = sorted(set(op["engine"] for op in seg))
        avail = {e: [] for e in engines}
        busy = {e: False for e in engines}
        order = []
        ev = []
        seq = 0
        for i in range(n):
            if npred[i] == 0:
                heapq.heappush(avail[seg[i]["engine"]], (-prio[i], i))

        def dispatch(e, now):
            nonlocal seq
            if busy[e] or not avail[e]:
                return
            _, i = heapq.heappop(avail[e])
            busy[e] = True
            order.append(i)
            if seg[i]["dma"]:
                occ, comp = 60.0, 60.0 + cost[i]
            else:
                occ = comp = cost[i]
            seq += 1
            heapq.heappush(ev, (now + occ, seq, "free", e))
            seq += 1
            heapq.heappush(ev, (now + comp, seq, "done", i))

        for e in engines:
            dispatch(e, 0.0)
        while len(order) < n:
            t, _, kind, pl = heapq.heappop(ev)
            if kind == "free":
                busy[pl] = False
                dispatch(pl, t)
            else:
                for s in succs[pl]:
                    npred[s] -= 1
                    if npred[s] == 0:
                        es = seg[s]["engine"]
                        heapq.heappush(avail[es], (-prio[s], s))
                        dispatch(es, t)
        return [seg[i] for i in order]

    def resolve(self):
        pos = {e: 0 for e in ENGINES}
        last_compute = {e: -1 for e in ENGINES}
        dmacnt = {}
        ledger = {}
        vec = {e: {} for e in ENGINES}
        hist = {e: [] for e in ENGINES}
        self.wait_targets = {e: set() for e in ENGINES}

        def need(t, eng):
            if t[0] == "eng":
                _, f, p = t
                if f == eng and eng not in SELF_SYNC:
                    return None
                if vec[eng].get(f, -1) >= p:
                    return None
            else:
                _, q, c = t
                if vec[eng].get(q, 0) >= c:
                    return None
            return t

        def apply_wait(eng, t):
            if t[0] == "eng":
                _, f, p = t
                self.wait_targets[f].add(p)
                if vec[eng].get(f, -1) < p:
                    vec[eng][f] = p
                    h = hist[f]
                    if p < len(h):
                        for k, v in h[p].items():
                            if vec[eng].get(k, -1 if isinstance(v, int) else 0) < v:
                                vec[eng][k] = v
            else:
                _, q, c = t
                if vec[eng].get(q, 0) < c:
                    vec[eng][q] = c

        for op in self.ops:
            if op.get("barrier"):
                op["bwaits"] = {}
                for e in ENGINES:
                    w = []
                    for f in ENGINES:
                        if last_compute[f] >= 0 and (f != e or e in SELF_SYNC):
                            t = need(("eng", f, last_compute[f]), e)
                            if t:
                                w.append(t)
                    for q, c in dmacnt.items():
                        if c > 0:
                            t = need(("dma", q, c), e)
                            if t:
                                w.append(t)
                    op["bwaits"][e] = w
                    for t in w:
                        apply_wait(e, t)
                continue
            eng = op["engine"]
            ps_reads = tuple(k for k in op["reads"] if k.startswith("ps")
                             and k not in op["writes"])
            eff_writes = op["writes"] + ps_reads
            eff_reads = tuple(k for k in op["reads"] if not k.startswith("ps"))
            waits = {}
            for r in eff_reads:
                ent = ledger.get(r)
                if ent and ent["w"] is not None:
                    t = need(ent["w"], eng)
                    if t:
                        k = t[1]
                        if k not in waits or waits[k][2] < t[2]:
                            waits[k] = t
            for wkey in eff_writes:
                ent = ledger.get(wkey)
                if ent:
                    cands = ([ent["w"]] if ent["w"] is not None else []) + ent["r"]
                    for c in cands:
                        t = need(c, eng)
                        if t:
                            k = t[1]
                            if k not in waits or waits[k][2] < t[2]:
                                waits[k] = t
            op["waits"] = list(waits.values())
            for t in op["waits"]:
                apply_wait(eng, t)
            p = pos[eng]
            if op["dma"]:
                q = op["dma"]
                dmacnt[q] = dmacnt.get(q, 0) + 16
                mytok = ("dma", q, dmacnt[q])
            else:
                mytok = ("eng", eng, p)
                last_compute[eng] = p
            for wkey in eff_writes:
                ledger[wkey] = {"w": mytok, "r": []}
            for r in eff_reads:
                ledger.setdefault(r, {"w": None, "r": []})["r"].append(mytok)
            op["pos"] = p
            hist[eng].append(dict(vec[eng]))
            pos[eng] += 1
        self.final_dma = dict(dmacnt)
        self.flagged = {e: sorted(self.wait_targets[e]) for e in ENGINES}

    def waitval(self, f, p):
        import bisect
        return bisect.bisect_right(self.flagged[f], p)


def _pack_chunks(w, c):
    n = w.shape[1]
    return np.ascontiguousarray(w.reshape(c, 128, n).transpose(1, 0, 2))


def build_program(debug=False, phases='ABC'):
    nc = bass.Bass("TRN2", target_bir_lowering=False)

    xd = nc.dram_tensor("xseq", [S, D], F32, kind="ExternalInput").ap()
    w_dq = nc.dram_tensor("w_dq", [128, 8, CP], F32, kind="ExternalInput").ap()
    w_uq = nc.dram_tensor("w_uq", [128, 2, D], F32, kind="ExternalInput").ap()
    w_dkv = nc.dram_tensor("w_dkv", [128, 8, CP], F32, kind="ExternalInput").ap()
    w_ukv = nc.dram_tensor("w_ukv", [128, 2, 2 * D], F32, kind="ExternalInput").ap()
    w_ot = nc.dram_tensor("w_ot", [128, 8, D], F32, kind="ExternalInput").ap()
    w_q = nc.dram_tensor("w_q", [128, 8, D], F32, kind="ExternalInput").ap()
    w_rt = nc.dram_tensor("w_rt", [128, 8, 255], F32, kind="ExternalInput").ap()
    w_lvt = nc.dram_tensor("w_lvt", [128, 8, 256], F32, kind="ExternalInput").ap()
    w_outt = nc.dram_tensor("w_outt", [128, 8, D], F32, kind="ExternalInput").ap()
    w_outb = nc.dram_tensor("w_outb", [1, D], F32, kind="ExternalInput").ap()
    vecs = nc.dram_tensor("vecs", [128, 8], F32, kind="ExternalInput").ap()
    mask4 = nc.dram_tensor("mask4", [128, 512], F32, kind="ExternalInput").ap()
    identb = nc.dram_tensor("identb", [128, 128], F32, kind="ExternalInput").ap()
    onesb = nc.dram_tensor("onesb", [1, 128], F32, kind="ExternalInput").ap()
    outd = nc.dram_tensor("out", [OWN * 128, D], F32, kind="ExternalOutput").ap()
    dbg_t = {}
    if debug:
        dbg_t["ckvf"] = nc.dram_tensor("dbg_ckvf", [128, 2 * S], BF16, kind="ExternalOutput").ap()
        dbg_t["cqf"] = nc.dram_tensor("dbg_cqf", [128, 2 * 2048], BF16, kind="ExternalOutput").ap()
        dbg_t["kpair"] = nc.dram_tensor("dbg_kpair", [128, S], BF16, kind="ExternalOutput").ap()
        dbg_t["vpair"] = nc.dram_tensor("dbg_vpair", [128, NT * 260], BF16, kind="ExternalOutput").ap()
        dbg_t["qpair"] = nc.dram_tensor("dbg_qpair", [128, 2048], BF16, kind="ExternalOutput").ap()
        dbg_t["oall"] = nc.dram_tensor("dbg_oall", [128, 8 * 2048], BF16, kind="ExternalOutput").ap()

    SS = Sched()
    global LAST_SS
    LAST_SS = SS
    alive = {}

    def sb(name, shape, dtype):
        g = nc.sbuf_tensor(name, shape, dtype)
        h = g.__enter__()
        alive[name] = g
        return h

    def psum(name, shape, dtype):
        g = nc.psum_tensor(name, shape, dtype)
        h = g.__enter__()
        alive[name] = g
        return h

    def free(*names):
        for n in reversed(names):
            alive.pop(n).__exit__(None, None, None)

    ident_b = sb("ident_b", [128, 128], BF16)
    mask_sb = sb("mask_sb", [128, 512], BF16)
    ones_b = sb("ones_b", [1, 128], BF16)
    vec_sb = sb("vec_sb", [128, 8], F32)
    woutb_b = sb("woutb_b", [1, D], BF16)
    stat = sb("stat", [128, 102], F32)
    wuq = sb("wuq", [128, 2 * D], BF16)
    wukv = sb("wukv", [128, 2 * 2 * D], BF16)
    wot = sb("wot", [128, 8 * D], BF16)
    wgT = sb("wgT", [128, 8 * 255], BF16)
    w2 = sb("w2", [128, 2 * D], BF16)
    o_all = sb("o_all", [128, 8 * 2048], BF16)
    ckvf = sb("ckvf", [128, 2 * S], BF16)
    cqf = sb("cqf", [128, 2 * 2048], BF16)
    stage = [sb("stage0", [128, 1024], F32), sb("stage1", [128, 1024], F32)]
    stageg = [sb("stageg0", [128, 1024], F32), sb("stageg1", [128, 1024], F32)]

    def wload(dst_sb, src_ap, total, tag, queue="sp"):
        stgs = stageg if queue == "gp" else stage
        pfx = "stgg" if queue == "gp" else "stg"
        if queue == "gp":
            queue = "sp"
        chunks = [(i, min(1024, total - i)) for i in range(0, total, 1024)]
        for idx, (off, ln) in enumerate(chunks):
            stg = stgs[idx % 2]
            stgk = f"{pfx}{idx % 2}"
            src = src_ap.rearrange("p a b -> p (a b)")[:, off:off + ln]

            def dfn(e, stg=stg, src=src, ln=ln):
                return e.dma_start(stg[:, 0:ln], src)

            def cfn(e, stg=stg, dst=dst_sb, off=off, ln=ln):
                return e.tensor_copy(dst[:, off:off + ln], stg[:, 0:ln])

            SS.dma(queue, dfn, reads=(stgk,), writes=(stgk,), sem=stgk,
                   dbg=f"ld_{tag}{idx}", cost=ln * 1.3 + 1000)
            SS.add("dve", cfn, reads=(stgk,), writes=(tag,), dbg=f"cast_{tag}{idx}",
                   cost=ln * 0.7 + 100)

    xre = xd.rearrange("(t p) d -> p t d", p=128)

    SS.dma("sp", lambda e: e.dma_start(stage[0][:, 0:128], identb),
           reads=("stg0",), writes=("stg0",), sem="stg0")
    SS.add("dve", lambda e: e.tensor_copy(ident_b[:], stage[0][:, 0:128]),
           reads=("stg0",), writes=("ident_b",))
    SS.dma("sp", lambda e: e.dma_start(stage[1][:, 0:512], mask4),
           reads=("stg1",), writes=("stg1",), sem="stg1")
    SS.add("dve", lambda e: e.tensor_copy(mask_sb[:], stage[1][:, 0:512]),
           reads=("stg1",), writes=("mask_sb",))
    SS.dma("sp", lambda e: e.dma_start(stage[0][0:1, 128:256], onesb),
           reads=("stg0",), writes=("stg0",), sem="stg0")
    SS.add("dve", lambda e: e.tensor_copy(ones_b[:], stage[0][0:1, 128:256]),
           reads=("stg0",), writes=("ones_b",))
    SS.dma("sp", lambda e: e.dma_start(vec_sb[:], vecs), writes=("vec_sb",),
           sem="once")
    SS.dma("sp", lambda e: e.dma_start(woutb_f[:], w_outb), writes=("woutb_f",),
           sem="once")
    SS.add("dve", lambda e: e.tensor_copy(woutb_b[:], woutb_f[:]),
           reads=("woutb_f",), writes=("woutb_b",))

    wdq = sb("wdq", [128, 8 * CP], BF16)
    wdkv = sb("wdkv", [128, 8 * CP], BF16)
    wq = sb("wq", [128, 8 * D], BF16)
    wrt = sb("wrt", [128, 8 * 255], BF16)
    wlvt = sb("wlvt", [128, 8 * 256], BF16)
    woutt = sb("woutt", [128, 8 * D], BF16)
    woutb_f = sb("woutb_f", [1, D], F32)
    xstg = [sb("xstg0", [128, 2 * D], F32), sb("xstg1", [128, 2 * D], F32)]
    sqs = [sb(f"sqs{i}", [128, D], F32) for i in range(3)]
    xnb = [sb(f"xnb{i}", [128, D], BF16) for i in range(3)]
    xnf = [sb(f"xnf{i}", [128, D], BF16) for i in range(3)]
    ps_tr = [psum(f"ps_tr{i}", [128, 1024], BF16) for i in range(3)]
    ps_ckv = [psum(f"ps_ckv{i}", [128, 512], F32) for i in range(3)]
    ps_cq = psum("ps_cq", [128, 512], F32)
    ps_pw = psum("ps_pw", [128, 512], F32)

    SS.dma("sp", lambda e: e.dma_start(
        xstg[0][:].rearrange("p (t d) -> p t d", t=2), xre[:, 0:2, :]),
        reads=("xstg_0",), writes=("xstg_0",), sem="xstg_0", dbg="ldx_0")
    wload(wdkv, w_dkv, 8 * CP, "wdkv", queue="gp")
    wload(wukv, w_ukv, 2 * 2 * D, "wukv", queue="gp")
    wload(wdq, w_dq, 8 * CP, "wdq", queue="gp")
    wload(wuq, w_uq, 2 * D, "wuq", queue="gp")

    def ln_stats(tag, slot, src_key, src_ap_fn, fd, sq_buf, sq_key,
                 extra_reads=()):
        c0 = SLOT[slot]
        k = slot + "_"

        def f_sum(e, src_ap_fn=src_ap_fn, c0=c0):
            return e.reduce_sum(stat[:, c0:c0 + 1], src_ap_fn(), AX.X)

        def f_sq(e, src_ap_fn=src_ap_fn, c0=c0, sq_buf=sq_buf, fd=fd):
            return e.activation(sq_buf[:, 0:fd], src_ap_fn(), AF.Square,
                                accum_out=stat[:, c0 + 1:c0 + 2])

        inv_n = 1.0 / fd

        def f_mean(e, c0=c0, inv_n=inv_n):
            return e.tensor_scalar(stat[:, c0 + 2:c0 + 3], stat[:, c0:c0 + 1],
                                   inv_n, None, ALU.mult)

        def f_msq(e, c0=c0):
            return e.tensor_scalar(stat[:, c0 + 3:c0 + 4], stat[:, c0 + 2:c0 + 3],
                                   stat[:, c0 + 2:c0 + 3], EPS,
                                   ALU.mult, ALU.subtract)

        def f_var(e, c0=c0, inv_n=inv_n):
            return e.scalar_tensor_tensor(stat[:, c0 + 3:c0 + 4],
                                          stat[:, c0 + 1:c0 + 2], inv_n,
                                          stat[:, c0 + 3:c0 + 4],
                                          ALU.mult, ALU.subtract)

        def f_ln(e, c0=c0):
            return e.activation(stat[:, c0 + 4:c0 + 5], stat[:, c0 + 3:c0 + 4],
                                AF.Ln)

        def f_ex(e, c0=c0):
            return e.activation(stat[:, c0 + 4:c0 + 5], stat[:, c0 + 4:c0 + 5],
                                AF.Exp, scale=-0.5)

        def f_nmr(e, c0=c0):
            return e.tensor_scalar(stat[:, c0 + 5:c0 + 6], stat[:, c0 + 2:c0 + 3],
                                   stat[:, c0 + 4:c0 + 5], -1.0, ALU.mult, ALU.mult)

        rd = (src_key,) + tuple(extra_reads)
        ops = [("dve", f_sum, rd, (k + "s",), f"{tag}_sum", fd * 1.05 + 150),
               ("act", f_sq, rd, (sq_key, k + "q"), f"{tag}_sq", fd * 0.85 + 300),
               ("dve", f_mean, (k + "s",), (k + "m",), f"{tag}_mean", 80),
               ("dve", f_msq, (k + "m",), (k + "v",), f"{tag}_msq", 80),
               ("dve", f_var, (k + "q", k + "v"), (k + "v",), f"{tag}_var", 80),
               ("act", f_ln, (k + "v",), (k + "r",), f"{tag}_ln", 300),
               ("act", f_ex, (k + "r",), (k + "r",), f"{tag}_exp", 300),
               ("dve", f_nmr, (k + "m", k + "r"), (k + "nmr",), f"{tag}_nmr", 80)]
        for eng_, fn_, rd_, wr_, dbg_, c_ in ops:
            SS.add(eng_, fn_, reads=rd_, writes=wr_, dbg=dbg_, cost=c_)
        return (stat[:, c0 + 4:c0 + 5], stat[:, c0 + 5:c0 + 6], k)

    nt_emit = NT
    lvl = 99
    if phases.startswith("A") and len(phases) > 1:
        parts = phases[1:].split("x")
        if parts[0].isdigit():
            nt_emit = int(parts[0])
        if len(parts) > 1 and parts[1].isdigit():
            lvl = int(parts[1])
    for t in range(nt_emit):
        par = t % 3
        j = t // 2
        if t % 2 == 0 and t > 0:
            def fx(e, j=j, b=xstg[j % 2]):
                return e.dma_start(b[:].rearrange("p (t d) -> p t d", t=2),
                                   xre[:, 2 * j:2 * (j + 1), :])
            SS.dma("sp", fx, reads=(f"xstg_{j % 2}",), writes=(f"xstg_{j % 2}",),
                   sem=f"xstg_{j % 2}", dbg=f"ldx_{j}", cost=3300)
        xbuf = xstg[j % 2]
        xk = f"xstg_{j % 2}"

        def xview(xbuf=xbuf, t=t):
            return xbuf[:, D * (t % 2):D * (t % 2 + 1)]

        if lvl < 1:
            continue
        sc, bi, k1 = ln_stats(f"ln1_{t}", f"ln1_{par}", xk, xview, D,
                              sqs[par], f"sqs{par}")

        def f_ap(e, xview=xview, o=xnb[par], sc=sc, bi=bi):
            return e.activation(o[:], xview(), AF.Identity, bias=bi, scale=sc)
        if lvl < 2:
            continue
        SS.add("act", f_ap, reads=(xk, k1 + "r", k1 + "nmr"),
               writes=(f"xnb{par}",), dbg=f"ln1ap_{t}", cost=1150)

        if lvl < 3:
            continue
        for c in range(8):
            def f_tr(e, c=c, i=xnb[par], o=ps_tr[par]):
                return e.transpose(o[:, 128 * c:128 * (c + 1)],
                                   i[:, 128 * c:128 * (c + 1)], ident_b[:])
            SS.add("pe", f_tr, reads=(f"xnb{par}", "ident_b"),
                   writes=(f"pstr{par}",), dbg=f"xnT_{t}_{c}", cost=320)

        def f_ev(e, i=ps_tr[par], o=xnf[par]):
            return e.tensor_copy(o[:], i[:])
        SS.add("dve", f_ev, reads=(f"pstr{par}",), writes=(f"xnf{par}",),
               dbg=f"xnTe_{t}", cost=850)
        if lvl < 4:
            continue

        for c in range(8):
            def f_mm(e, c=c, i=xnf[par], o=ps_ckv[par]):
                return e.matmul(o[:, 0:CP], i[:, 128 * c:128 * (c + 1)],
                                wdkv[:, CP * c:CP * (c + 1)],
                                start=(c == 0), stop=(c == 7))
            SS.add("pe", f_mm, reads=(f"xnf{par}", "wdkv"),
                   writes=(f"psckv{par}",), dbg=f"ckvmm_{t}_{c}", cost=190)

        if lvl < 5:
            continue
        sck, bik, k2 = ln_stats(f"kv_{t}", f"kv_{par}", f"psckv{par}",
                                (lambda o=ps_ckv[par]: o[:, 0:CP]), CP,
                                sqs[par], f"sqs{par}")

        def f_apk(e, o=xnb[par], i=ps_ckv[par], sck=sck, bik=bik):
            return e.activation(o[:, 0:CP], i[:, 0:CP], AF.Identity,
                                bias=bik, scale=sck)
        if lvl < 6:
            continue
        SS.add("act", f_apk, reads=(f"psckv{par}", k2 + "r", k2 + "nmr"),
               writes=(f"xnb{par}",), dbg=f"kvap_{t}", cost=500)
        if lvl < 7:
            continue
        for c in range(2):
            def f_trk(e, c=c, i=xnb[par], o=ps_tr[par]):
                return e.transpose(o[:, 128 * c:128 * (c + 1)],
                                   i[:, 128 * c:128 * (c + 1)], ident_b[:])
            SS.add("pe", f_trk, reads=(f"xnb{par}", "ident_b"),
                   writes=(f"pstr{par}",), dbg=f"ckvT_{t}_{c}", cost=320)
        for c in range(2):
            def f_evk(e, c=c, i=ps_tr[par], t=t):
                return e.tensor_scalar(ckvf[:, S * c + 128 * t:S * c + 128 * (t + 1)],
                                       i[:, 128 * c:128 * (c + 1)],
                                       vec_sb[:, 4 + c:5 + c], vec_sb[:, 6 + c:7 + c],
                                       ALU.mult, ALU.add)
            SS.add("dve", f_evk, reads=(f"pstr{par}", "vec_sb"),
                   writes=(f"ckvf{t // 4}",), dbg=f"ckvev_{t}_{c}", cost=280)

        if lvl < 8:
            continue
        if t % 2 == 0:
            i_own = t // 2
            pcq = ps_cq if i_own % 2 == 0 else ps_pw
            pcqk = "pscq" if i_own % 2 == 0 else "pspw"
            for c in range(8):
                def f_mmq(e, c=c, i=xnf[par], pcq=pcq):
                    return e.matmul(pcq[:, 0:CP], i[:, 128 * c:128 * (c + 1)],
                                    wdq[:, CP * c:CP * (c + 1)],
                                    start=(c == 0), stop=(c == 7))
                SS.add("pe", f_mmq, reads=(f"xnf{par}", "wdq"),
                       writes=(pcqk,), dbg=f"cqmm_{t}_{c}", cost=190)
            scq, biq, k3 = ln_stats(f"cq_{t}", f"cq_{(t // 2) % 2}", pcqk,
                                    (lambda pcq=pcq: pcq[:, 0:CP]), CP,
                                    sqs[par], f"sqs{par}")

            def f_apq(e, o=xnb[par], scq=scq, biq=biq, pcq=pcq):
                return e.activation(o[:, 0:CP], pcq[:, 0:CP], AF.Identity,
                                    bias=biq, scale=scq)
            SS.add("act", f_apq, reads=(pcqk, k3 + "r", k3 + "nmr"),
                   writes=(f"xnb{par}",), dbg=f"cqap_{t}", cost=500)
            for c in range(2):
                def f_trq(e, c=c, i=xnb[par], o=ps_tr[par]):
                    return e.transpose(o[:, 128 * c:128 * (c + 1)],
                                       i[:, 128 * c:128 * (c + 1)], ident_b[:])
                SS.add("pe", f_trq, reads=(f"xnb{par}", "ident_b"),
                       writes=(f"pstr{par}",), dbg=f"cqT_{t}_{c}", cost=320)
            for c in range(2):
                def f_evq(e, c=c, i=ps_tr[par], i_own=i_own):
                    return e.tensor_scalar(
                        cqf[:, 2048 * c + 128 * i_own:2048 * c + 128 * (i_own + 1)],
                        i[:, 128 * c:128 * (c + 1)],
                        vec_sb[:, 0 + c:1 + c], vec_sb[:, 2 + c:3 + c],
                        ALU.mult, ALU.add)
                SS.add("dve", f_evq, reads=(f"pstr{par}", "vec_sb"),
                       writes=(f"cqf{i_own // 4}",), dbg=f"cqev_{t}_{c}", cost=280)

    wload(wq, w_q, 8 * D, "wq")
    wload(wrt, w_rt, 8 * 255, "wrt")
    wload(wlvt, w_lvt, 8 * 256, "wlvt")
    wload(woutt, w_outt, 8 * D, "woutt")
    for dsub in range(8):
        for ec in range(8):
            def f_pg(e, dsub=dsub, ec=ec):
                return e.matmul(ps_pw[:, 0:255],
                                wq[:, D * ec + 128 * dsub:D * ec + 128 * (dsub + 1)],
                                wrt[:, 255 * ec:255 * (ec + 1)],
                                start=(ec == 0), stop=(ec == 7))
            SS.add("pe", f_pg, reads=("wq", "wrt"), writes=("pspw",),
                   dbg=f"pg_{dsub}_{ec}", cost=190)

        def f_pge(e, dsub=dsub):
            return e.tensor_copy(wgT[:, 255 * dsub:255 * (dsub + 1)], ps_pw[:, 0:255])
        SS.add("dve", f_pge, reads=("pspw",), writes=("wgT",), dbg=f"pge_{dsub}",
               cost=370)
    for lc in range(2):
        for h in range(2):
            for dc in range(8):
                def f_p2(e, lc=lc, h=h, dc=dc):
                    return e.matmul(ps_pw[:, 0:512],
                                    wlvt[:, 256 * dc + 128 * lc:256 * dc + 128 * (lc + 1)],
                                    woutt[:, D * dc + 512 * h:D * dc + 512 * (h + 1)],
                                    start=(dc == 0), stop=(dc == 7))
                SS.add("pe", f_p2, reads=("wlvt", "woutt"), writes=("pspw",),
                       dbg=f"p2_{lc}_{h}_{dc}", cost=300)

            def f_p2e(e, lc=lc, h=h):
                return e.tensor_copy(w2[:, D * lc + 512 * h:D * lc + 512 * (h + 1)],
                                     ps_pw[:, 0:512])
            SS.add("dve", f_p2e, reads=("pspw",), writes=("w2",), dbg=f"p2e_{lc}_{h}",
                   cost=460)

    SS.barrier()
    free("ps_tr0", "ps_tr1", "ps_tr2", "ps_ckv0", "ps_ckv1", "ps_ckv2",
         "ps_cq", "ps_pw")
    free("wdq", "wdkv", "wq", "wrt", "wlvt", "woutt", "woutb_f",
         "xstg0", "xstg1", "sqs0", "sqs1", "sqs2", "xnb0", "xnb1", "xnb2",
         "xnf0", "xnf1", "xnf2")

    kp = [sb("kp0", [128, S], BF16), sb("kp1", [128, S], BF16)]
    vpg = [sb("vpg0", [128, NT * 260], BF16), sb("vpg1", [128, NT * 260], BF16)]
    qp = [sb("qp0", [128, 2048], BF16), sb("qp1", [128, 2048], BF16)]
    pb = [sb(f"pb{i}", [128, 1024], BF16) for i in range(4)]
    d_t = sb("d_t", [32, 1024], F32)
    d_ti = sb("d_ti", [32, 1024], F32)
    d_bc = sb("d_bc", [64, 1024], F32)
    ps_s = [psum("ps_s0", [128, 1024], F32), psum("ps_s1", [128, 1024], F32)]
    ps_av = psum("ps_av", [128, 1024], F32)
    ps_j = [psum("ps_j0", [128, 512], F32), psum("ps_j1", [128, 512], F32)]

    for b in range(2):
        def f_ones(e, b=b):
            return e.memset(vpg[b][:].rearrange("p (t q h c) -> p t q h c",
                                                t=NT, q=2, h=2)[:, :, :, :, 64:65], 1.0)
        SS.add("dve", f_ones, reads=(),
               writes=tuple(f"vpg{b}_{tg}" for tg in range(16)),
               dbg=f"vones_{b}")

    zrow = sb("zrow", [1, 512], BF16)
    SS.add("dve", lambda e: e.memset(zrow[:], 0.0), writes=("zrow",), dbg="zrow")
    SS.add("dve", lambda e: e.memset(ps_av[64:96, :], 0.0), writes=("psav",),
           dbg="avinit")
    SS.add("dve", lambda e: e.memset(d_ti[0:32, :], 0.0), writes=("d_ti",), dbg="dti0")

    wload(wot, w_ot, 8 * D, "wot")

    def jit_ops(p):
        b = p % 2
        ops = []
        for kc in range(8):
            for c in range(2):
                def f(e, p=p, b=b, kc=kc, c=c):
                    return e.matmul(ps_j[b][:, 0:512],
                                    wukv[:, 2 * D * c + 128 * p:2 * D * c + 128 * (p + 1)],
                                    ckvf[:, S * c + 512 * kc:S * c + 512 * (kc + 1)],
                                    start=(c == 0), stop=(c == 1))
                ops.append(("pe", f, ("wukv", f"ckvf{kc}"), (f"psj{b}",), f"kj_{p}_{kc}_{c}", 300))

            def fe(e, b=b, kc=kc):
                return e.tensor_copy(kp[b][:, 512 * kc:512 * (kc + 1)], ps_j[b][:, 0:512])
            ops.append(("dve", fe, (f"psj{b}",), (f"kp{b}_{kc}",), f"kje_{p}_{kc}", 460))
        for qc in range(4):
            for c in range(2):
                def f(e, p=p, b=b, qc=qc, c=c):
                    return e.matmul(ps_j[b][:, 0:512],
                                    wuq[:, D * c + 128 * p:D * c + 128 * (p + 1)],
                                    cqf[:, 2048 * c + 512 * qc:2048 * c + 512 * (qc + 1)],
                                    start=(c == 0), stop=(c == 1))
                ops.append(("pe", f, ("wuq", f"cqf{qc}"), (f"psj{b}",), f"qj_{p}_{qc}_{c}", 300))

            def fe(e, b=b, qc=qc):
                return e.tensor_scalar(qp[b][:, 512 * qc:512 * (qc + 1)],
                                       ps_j[b][:, 0:512], 0.125, None, ALU.mult)
            ops.append(("dve", fe, (f"psj{b}",), (f"qp{b}_{qc}",), f"qje_{p}_{qc}", 460))
        return ops

    def emit_jit_v(g):
        b = g % 2
        for tg in range(16):
            for tt in range(2):
                for c in range(2):
                    def f(e, g=g, b=b, tg=tg, tt=tt, c=c):
                        t_ = 2 * tg + tt
                        return e.matmul(
                            ps_j[b][:, 256 * tt:256 * (tt + 1)],
                            ckvf[:, S * c + 128 * t_:S * c + 128 * (t_ + 1)],
                            wukv[:, 2 * D * c + D + 256 * g:2 * D * c + D + 256 * (g + 1)],
                            start=(c == 0), stop=(c == 1))
                    SS.add("pe", f, reads=(f"ckvf{(2 * tg + tt) // 4}", "wukv"),
                           writes=(f"psj{b}",),
                           dbg=f"vj_{g}_{tg}_{tt}_{c}", cost=190)

            def fe(e, b=b, g=g, tg=tg):
                src = ps_j[b][:, 0:512].rearrange("p (t q h c) -> p t q h c",
                                                  t=2, q=2, h=2)
                dst = vpg[g % 2][:].rearrange("p (t q h c) -> p t q h c",
                                              t=NT, q=2, h=2)[:, 2 * tg:2 * (tg + 1), :, :, 0:64]
                return e.tensor_copy(dst, src)
            SS.add("dve", fe, reads=(f"psj{b}",), writes=(f"vpg{g % 2}_{tg}",),
                   dbg=f"vje_{g}_{tg}", cost=560)

    pending_jit = jit_ops(0) if 'B' in phases else []
    for o in pending_jit:
        SS.add(*o[0:2], reads=o[2], writes=o[3], dbg=o[4], cost=o[5])
    if 'B' in phases:
        emit_jit_v(0)
    pending_jit = jit_ops(1) if 'B' in phases else []

    avs = sb("avs", [128, 1024], F32)

    for p in (range(8) if 'B' in phases else []):
        b = p % 2
        for c in range(4):
            nkt = 8 * c + 8

            def emit_sc(kt, b=b, c=c, p=p):
                m = kt - 8 * c
                colstart = 128 * (m // 2) if m >= 0 else 0
                sp_ = ps_s[kt % 2]

                def f_sc(e, b=b, c=c, kt=kt, colstart=colstart, sp_=sp_):
                    for h2 in range(2):
                        ins = e.matmul(
                            sp_[:, 512 * h2 + colstart:512 * (h2 + 1)],
                            kp[b][64 * h2:64 * (h2 + 1), 128 * kt:128 * (kt + 1)],
                            qp[b][64 * h2:64 * (h2 + 1),
                                  512 * c + colstart:512 * (c + 1)],
                            start=True, stop=True, tile_position=(64 * h2, 0))
                    return ins
                SS.add("pe", f_sc, reads=(f"kp{b}_{kt // 4}", f"qp{b}_{c}"),
                       writes=(f"pss{kt % 2}",), dbg=f"sc_{p}_{c}_{kt}",
                       cost=(512 - colstart) * 0.45 + 170)

            def emit_tail(kt, b=b, c=c, p=p):
                m = kt - 8 * c
                colstart = 128 * (m // 2) if m >= 0 else 0
                sp_ = ps_s[kt % 2]
                pbx = pb[kt % 4]

                def f_ex(e, sp_=sp_, pbx=pbx, colstart=colstart):
                    iv = sp_[:].rearrange("p (h q) -> p h q", h=2)[:, :, colstart:512]
                    ov = pbx[:].rearrange("p (h q) -> p h q", h=2)[:, :, colstart:512]
                    return e.activation(ov, iv, AF.Exp)
                SS.add("act", f_ex, reads=(f"pss{kt % 2}",),
                       writes=(f"pb{kt % 4}",), dbg=f"exp_{p}_{c}_{kt}",
                       cost=(2 * (512 - colstart) + 352) / 1.2)

                if m >= 0:
                    moff = 128 * (m % 2)

                    def f_mk(e, pbx=pbx, colstart=colstart, moff=moff):
                        pv = pbx[:].rearrange("p (h q) -> p h q", h=2)[:, :, colstart:colstart + 128]
                        mv = mask_sb[:].rearrange("p (h q) -> p h q", h=2)[:, :, moff:moff + 128]
                        return e.tensor_tensor(pv, pv, mv, ALU.mult)
                    SS.add("dve", f_mk, reads=("mask_sb", f"pb{kt % 4}"),
                           writes=(f"pb{kt % 4}",), dbg=f"mask_{p}_{c}_{kt}",
                           cost=190)

                g2 = (p // 2) % 2
                q_ = p % 2
                for h2 in range(2):
                    def f_av(e, g2=g2, q_=q_, kt=kt, h2=h2, colstart=colstart, pbx=pbx):
                        return e.matmul(
                            ps_av[0:65, 512 * h2 + colstart:512 * (h2 + 1)],
                            vpg[g2][:, 260 * kt + 130 * q_ + 65 * h2:260 * kt + 130 * q_ + 65 * (h2 + 1)],
                            pbx[:, 512 * h2 + colstart:512 * (h2 + 1)],
                            start=(kt == 0), stop=False)
                    SS.add("pe", f_av, reads=(f"vpg{g2}_{kt // 2}", f"pb{kt % 4}"),
                           writes=("psav",), dbg=f"av_{p}_{c}_{kt}_{h2}",
                           cost=(512 - colstart) * 0.45 + 70)

                for _ in range(2):
                    if pending_jit:
                        o = pending_jit.pop(0)
                        SS.add(*o[0:2], reads=o[2], writes=o[3], dbg=o[4],
                               cost=o[5])

            emit_sc(0)
            for kt in range(1, nkt):
                emit_sc(kt)
                emit_tail(kt - 1)
            emit_tail(nkt - 1)

            for h2 in range(2):
                def f_cl(e, h2=h2):
                    return e.matmul(ps_av[0:65, 512 * h2:512 * (h2 + 1)],
                                    zrow[0:1, 0:65], zrow[0:1, 0:512],
                                    start=False, stop=True)
                SS.add("pe", f_cl, reads=("zrow",), writes=("psav",),
                       dbg=f"avcl_{p}_{c}_{h2}", cost=280)
            SS.add("dve", lambda e: e.tensor_copy(avs[0:96, :], ps_av[0:96, :]),
                   reads=("psav",), writes=("avs",), dbg=f"avev_{p}_{c}",
                   cost=1170)
            SS.add("dve", lambda e: e.transpose(d_t[0:32, :], avs[64:96, 0:1024]),
                   reads=("avs",), writes=("d_t",), dbg=f"dvt1_{p}_{c}",
                   cost=1200)

            def f_rc(e):
                iv = d_t[0:32, :].rearrange("p (a b) -> p a b", b=32)[:, :, 0:1]
                ov = d_ti[0:32, :].rearrange("p (a b) -> p a b", b=32)[:, :, 0:1]
                return e.reciprocal(ov, iv)
            SS.add("dve", f_rc, reads=("d_t",), writes=("d_ti",), dbg=f"drc_{p}_{c}",
                   cost=250)
            SS.add("dve", lambda e: e.transpose(d_t[0:32, :], d_ti[0:32, :]),
                   reads=("d_ti",), writes=("d_t",), dbg=f"dvt2_{p}_{c}",
                   cost=1200)
            SS.add("dve", lambda e: e.stream_shuffle(d_bc[0:32, :], d_t[0:32, :], [0] * 32),
                   reads=("d_t",), writes=("d_bc",), dbg=f"dsh1_{p}_{c}", cost=650)
            SS.add("dve", lambda e: e.stream_shuffle(d_bc[32:64, :], d_t[0:32, :], [0] * 32),
                   reads=("d_t",), writes=("d_bc",), dbg=f"dsh2_{p}_{c}", cost=650)
            for h2 in range(2):
                def f_oe(e, p=p, c=c, h2=h2):
                    return e.tensor_tensor(
                        o_all[64 * h2:64 * (h2 + 1), 2048 * p + 512 * c:2048 * p + 512 * (c + 1)],
                        avs[0:64, 512 * h2:512 * (h2 + 1)],
                        d_bc[:, 512 * h2:512 * (h2 + 1)], ALU.mult)
                SS.add("dve", f_oe, reads=("avs", "d_bc"), writes=("o_all",),
                       dbg=f"oev_{p}_{c}_{h2}", cost=460)
        if p < 7:
            for o in pending_jit:
                SS.add(*o[0:2], reads=o[2], writes=o[3], dbg=o[4], cost=o[5])
            pending_jit = jit_ops(p + 2) if p + 2 < 8 else []
        if p in (0, 1, 3):
            emit_jit_v({0: 1, 1: 2, 3: 3}[p])

    if debug:
        SS.dma("sp", lambda e: e.dma_start(dbg_t["ckvf"], ckvf[:]), reads=("ckvf",), sem="dbg", dbg="dbg1")
        SS.dma("sp", lambda e: e.dma_start(dbg_t["cqf"], cqf[:]), reads=("cqf",), sem="dbg", dbg="dbg2")
        SS.dma("sp", lambda e: e.dma_start(dbg_t["kpair"], kp[1][:]), reads=("kp1",), sem="dbg", dbg="dbg3")
        SS.dma("sp", lambda e: e.dma_start(dbg_t["vpair"], vpg[1][:]), reads=("vpg1",), sem="dbg", dbg="dbg4")
        SS.dma("sp", lambda e: e.dma_start(dbg_t["qpair"], qp[1][:]), reads=("qp1",), sem="dbg", dbg="dbg5")
        SS.dma("sp", lambda e: e.dma_start(dbg_t["oall"], o_all[:]), reads=("o_all",), sem="dbg", dbg="dbg6")

    SS.barrier()
    free("ps_s0", "ps_s1", "ps_av", "ps_j0", "ps_j1")
    free("kp0", "kp1", "vpg0", "vpg1", "qp0", "qp1", "pb0", "pb1", "pb2",
         "pb3", "d_t", "d_ti", "d_bc", "zrow", "avs")
    free("ckvf", "cqf", "stage0", "stage1", "stageg0", "stageg1")

    NPAR = 3
    xc = [sb(f"xc{i}", [128, D], F32) for i in range(NPAR)]
    attn_n = [sb(f"attn_n{i}", [128, D], F32) for i in range(NPAR)]
    x2 = [sb(f"x2_{i}", [128, D], F32) for i in range(NPAR)]
    xn2b = [sb(f"xn2b{i}", [128, D], BF16) for i in range(NPAR)]
    xn2f = [sb(f"xn2f{i}", [128, D], BF16) for i in range(NPAR)]
    texp = [sb(f"texp{i}", [128, 255], F32) for i in range(NPAR)]
    both = [sb(f"both{i}", [128, 510], F32) for i in range(NPAR)]
    trees2 = [[sb(f"treeA{i}", [128, 256], F32), sb(f"treeB{i}", [128, 256], F32)]
              for i in range(NPAR)]
    lpb = [sb(f"lpb{i}", [128, 256], BF16) for i in range(NPAR)]
    lpf = [sb(f"lpf{i}", [128, 256], BF16) for i in range(NPAR)]
    ffnn = [sb(f"ffnn{i}", [128, D], F32) for i in range(NPAR)]
    outt = [sb(f"outt{i}", [128, D], F32) for i in range(NPAR)]
    sqc = [sb(f"sqc{i}", [128, D], F32) for i in range(NPAR)]
    ps_c1 = [psum(f"ps_c1{i}", [128, 1024], F32) for i in range(NPAR)]
    ps_trb = [psum("ps_trb0", [128, 1024], BF16), psum("ps_trb1", [128, 1024], BF16)]

    for tau in (range(OWN) if 'C' in phases else []):
        par = tau % NPAR

        def fx(e, tau=tau, bx=xc[par]):
            return e.dma_start(bx[:], xre[:, 2 * tau, :])
        SS.dma("act", fx, reads=(f"xc_{par}",), writes=(f"xc_{par}",),
               sem=f"xc_{par}", dbg=f"ldxc_{tau}", cost=2500)
        xck = f"xc_{par}"

        def xcv(bx=xc[par]):
            return bx[:]

        pc1 = ps_c1[par]
        pc1k = f"psc1{par}"
        for p8 in range(8):
            for half in range(2):
                def f_mm(e, p8=p8, half=half, tau=tau, pc1=pc1):
                    return e.matmul(pc1[:, 512 * half:512 * (half + 1)],
                                    o_all[:, 2048 * p8 + 128 * tau:2048 * p8 + 128 * (tau + 1)],
                                    wot[:, D * p8 + 512 * half:D * p8 + 512 * (half + 1)],
                                    start=(p8 == 0), stop=(p8 == 7))
                SS.add("pe", f_mm, reads=("o_all", "wot"), writes=(pc1k,),
                       dbg=f"wo_{tau}_{p8}_{half}", cost=300)
        sca, bia, ka = ln_stats(f"attn_{tau}", f"attn_{par}", pc1k,
                                (lambda pc1=pc1: pc1[:]), D, sqc[par], f"sqc{par}")

        def f_apa(e, o=attn_n[par], pc1=pc1, sca=sca, bia=bia):
            return e.activation(o[:], pc1[:], AF.Identity, bias=bia, scale=sca)
        SS.add("act", f_apa, reads=(pc1k, ka + "r", ka + "nmr"),
               writes=(f"attn_n{par}",), dbg=f"attnap_{tau}", cost=1150)

        def f_x2(e, o=x2[par], a=attn_n[par], xcv=xcv):
            return e.tensor_tensor(o[:], a[:], xcv(), ALU.add)
        SS.add("dve", f_x2, reads=(f"attn_n{par}", xck), writes=(f"x2_{par}",),
               dbg=f"x2_{tau}", cost=1170)

        sc2, bi2, k2c = ln_stats(f"ln2_{tau}", f"ln2_{par}", f"x2_{par}",
                                 (lambda o=x2[par]: o[:]), D, sqc[par], f"sqc{par}")

        def f_ap2(e, o=xn2b[par], i=x2[par], sc2=sc2, bi2=bi2):
            return e.activation(o[:], i[:], AF.Identity, bias=bi2, scale=sc2)
        SS.add("act", f_ap2, reads=(f"x2_{par}", k2c + "r", k2c + "nmr"),
               writes=(f"xn2b{par}",), dbg=f"ln2ap_{tau}", cost=1150)

        ptr = ps_trb[tau % 2]
        ptrk = f"pstrb{tau % 2}"
        pgg = pc1
        pggk = pc1k
        trees = trees2[par]
        for cch in range(8):
            def f_t2(e, cch=cch, i=xn2b[par], ptr=ptr):
                return e.transpose(ptr[:, 128 * cch:128 * (cch + 1)],
                                   i[:, 128 * cch:128 * (cch + 1)], ident_b[:])
            SS.add("pe", f_t2, reads=(f"xn2b{par}", "ident_b"), writes=(ptrk,),
                   dbg=f"x2T_{tau}_{cch}", cost=320)
        SS.add("dve", (lambda o=xn2f[par], ptr=ptr: lambda e: e.tensor_copy(o[:], ptr[:]))(),
               reads=(ptrk,), writes=(f"xn2f{par}",), dbg=f"x2Te_{tau}", cost=700)

        for dc in range(8):
            def f_g(e, dc=dc, i=xn2f[par], pgg=pgg):
                return e.matmul(pgg[:, 0:255], i[:, 128 * dc:128 * (dc + 1)],
                                wgT[:, 255 * dc:255 * (dc + 1)],
                                start=(dc == 0), stop=(dc == 7))
            SS.add("pe", f_g, reads=(f"xn2f{par}", "wgT"), writes=(pggk,),
                   dbg=f"gate_{tau}_{dc}", cost=190)

        def f_te(e, o=texp[par], pgg=pgg):
            return e.activation(o[:], pgg[:, 0:255], AF.Exp, scale=-1.0)
        SS.add("act", f_te, reads=(pggk,), writes=(f"texp{par}",), dbg=f"texp_{tau}",
               cost=500)

        bview = both[par][:].rearrange("p (a b) -> p a b", b=2)

        def f_lg(e, o=bview, i=texp[par]):
            return e.activation(o[:, :, 1:2], i[:].rearrange("p (a b) -> p a b", b=1),
                                AF.Ln, bias=1.0)
        SS.add("act", f_lg, reads=(f"texp{par}",), writes=(f"both{par}",),
               dbg=f"lg_{tau}", cost=500)

        def f_gx(e, o=bview):
            return e.activation(o[:, :, 1:2], o[:, :, 1:2], AF.Exp, scale=-1.0)
        SS.add("act", f_gx, reads=(f"both{par}",), writes=(f"both{par}",),
               dbg=f"gx_{tau}", cost=500)

        def f_omg(e, o=bview, a=texp[par]):
            return e.tensor_tensor(o[:, :, 0:1],
                                   a[:].rearrange("p (a b) -> p a b", b=1),
                                   o[:, :, 1:2], ALU.mult)
        SS.add("dve", f_omg, reads=(f"texp{par}", f"both{par}"),
               writes=(f"both{par}",), dbg=f"omg_{tau}", cost=370)
        SS.add("dve", (lambda o=trees[0], i=both[par]:
                       lambda e: e.tensor_copy(o[:, 0:2], i[:, 0:2]))(),
               reads=(f"both{par}",), writes=(f"treeA{par}",), dbg=f"tr0_{tau}",
               cost=80)
        cur = 0
        for lv in range(1, 8):
            n = 1 << lv
            nxt = 1 - cur

            def f_lv(e, cur=cur, nxt=nxt, n=n, par=par, trees=trees):
                ov = trees[nxt][:, 0:2 * n].rearrange("p (a b) -> p a b", b=2)
                iv = trees[cur][:, 0:n].rearrange("p (a b) -> p a b", b=1)
                gv = both[par][:, 2 * (n - 1):2 * (2 * n - 1)].rearrange(
                    "p (a b) -> p a b", b=2)
                return e.tensor_tensor(ov, iv.broadcast_to((128, n, 2)), gv,
                                       ALU.mult)
            SS.add("dve", f_lv,
                   reads=((f"treeA{par}" if cur == 0 else f"treeB{par}"),
                          f"both{par}"),
                   writes=(f"treeB{par}" if nxt == 1 else f"treeA{par}",),
                   dbg=f"tr{lv}_{tau}", cost=120 + 2 * n)
            cur = nxt
        lpsrc = trees[cur]
        lpsrck = (f"treeA{par}" if cur == 0 else f"treeB{par}")
        SS.add("dve", (lambda o=lpb[par], i=lpsrc: lambda e: e.tensor_copy(o[:], i[:]))(),
               reads=(lpsrck,), writes=(f"lpb{par}",), dbg=f"lpb_{tau}", cost=370)
        for cch in range(2):
            def f_tl(e, cch=cch, i=lpb[par], ptr=ptr):
                return e.transpose(ptr[:, 128 * cch:128 * (cch + 1)],
                                   i[:, 128 * cch:128 * (cch + 1)], ident_b[:])
            SS.add("pe", f_tl, reads=(f"lpb{par}", "ident_b"), writes=(ptrk,),
                   dbg=f"lpT_{tau}_{cch}", cost=320)
        SS.add("dve", (lambda o=lpf[par], ptr=ptr: lambda e: e.tensor_copy(o[:], ptr[:, 0:256]))(),
               reads=(ptrk,), writes=(f"lpf{par}",), dbg=f"lpTe_{tau}", cost=370)
        for lc in range(2):
            for half in range(2):
                def f_of(e, lc=lc, half=half, i=lpf[par], pc1=pc1):
                    return e.matmul(pc1[:, 512 * half:512 * (half + 1)],
                                    i[:, 128 * lc:128 * (lc + 1)],
                                    w2[:, D * lc + 512 * half:D * lc + 512 * (half + 1)],
                                    start=(lc == 0), stop=False)
                SS.add("pe", f_of, reads=(f"lpf{par}", "w2"), writes=(pc1k,),
                       dbg=f"of_{tau}_{lc}_{half}", cost=300)
        for half in range(2):
            def f_bias(e, half=half, pc1=pc1):
                return e.matmul(pc1[:, 512 * half:512 * (half + 1)], ones_b[:],
                                woutb_b[:, 512 * half:512 * (half + 1)],
                                start=False, stop=True)
            SS.add("pe", f_bias, reads=("ones_b", "woutb_b"), writes=(pc1k,),
                   dbg=f"ofb_{tau}_{half}", cost=300)
        scf, bif, kf = ln_stats(f"ffn_{tau}", f"ffn_{par}", pc1k,
                                (lambda pc1=pc1: pc1[:]), D, sqc[par], f"sqc{par}")

        def f_apf(e, o=ffnn[par], pc1=pc1, scf=scf, bif=bif):
            return e.activation(o[:], pc1[:], AF.Identity, bias=bif, scale=scf)
        SS.add("act", f_apf, reads=(pc1k, kf + "r", kf + "nmr"),
               writes=(f"ffnn{par}",), dbg=f"ffnap_{tau}", cost=1150)

        def f_out(e, o=outt[par], a=ffnn[par], b_=x2[par]):
            return e.tensor_tensor(o[:], a[:], b_[:], ALU.add)
        SS.add("dve", f_out, reads=(f"ffnn{par}", f"x2_{par}"),
               writes=(f"outt{par}",), dbg=f"out_{tau}", cost=1170)

        def f_st(e, tau=tau, o=outt[par]):
            return e.dma_start(outd[128 * tau:128 * (tau + 1), :], o[:])
        SS.dma("sp", f_st, reads=(f"outt{par}",), writes=(),
               sem=f"outt{par}", dbg=f"st_{tau}", cost=2500)

    import os
    if os.environ.get("NOREORDER", "") != "1":
        SS.reorder()
    SS.resolve()

    sems = {}
    semguards = []
    for e in ENGINES:
        g = nc.semaphore(f"prog_{e}")
        sems[e] = g.__enter__()
        semguards.append(g)
    for q in sorted(SS.final_dma):
        g = nc.semaphore(q)
        sems[q] = g.__enter__()
        semguards.append(g)

    blk_g = nc.Block()
    blk = blk_g.__enter__()

    def run_engine(ename):
        def body(eng):
            for op in SS.ops:
                if op.get("barrier"):
                    for t in op["bwaits"].get(ename, []):
                        if t[0] == "eng":
                            v = SS.waitval(t[1], t[2])
                            if v > 0:
                                eng.wait_ge(sems[t[1]], v)
                        else:
                            eng.wait_ge(sems[t[1]], t[2])
                    continue
                if op["engine"] != ename:
                    continue
                for t in op["waits"]:
                    if t[0] == "eng":
                        v = SS.waitval(t[1], t[2])
                        if v > 0:
                            eng.wait_ge(sems[t[1]], v)
                    else:
                        eng.wait_ge(sems[t[1]], t[2])
                ins = op["fn"](eng)
                if op["dma"]:
                    ins.then_inc(sems[op["dma"]], 16)
                elif op["pos"] in SS.wait_targets[ename]:
                    ins.then_inc(sems[ename], 1)
            if ename == "sp":
                for q, c in SS.final_dma.items():
                    if c > 0:
                        eng.wait_ge(sems[q], c)
        return body

    blk.tensor(run_engine("pe"))
    blk.scalar(run_engine("act"))
    blk.vector(run_engine("dve"))
    blk.sync(run_engine("sp"))
    blk_g.__exit__(None, None, None)
    for n in reversed(list(alive)):
        alive.pop(n).__exit__(None, None, None)
    for g in semguards:
        g.__exit__(None, None, None)
    return nc


def make_in_maps(inputs, debug=False):
    x = np.asarray(inputs["L0_attn_x"], dtype=np.float32)
    tri = np.triu(np.ones((128, 128), np.float32))
    import ml_dtypes

    def bf(a):
        return np.ascontiguousarray(a).astype(ml_dtypes.bfloat16)

    common = {
        "w_dq": _pack_chunks(np.asarray(inputs["L0_attn_W_dq"], np.float32), 8),
        "w_uq": _pack_chunks(np.asarray(inputs["L0_attn_W_uq"], np.float32), 2),
        "w_dkv": _pack_chunks(np.asarray(inputs["L0_attn_W_dkv"], np.float32), 8),
        "w_ukv": _pack_chunks(np.asarray(inputs["L0_attn_W_ukv"], np.float32), 2),
        "w_ot": _pack_chunks(np.ascontiguousarray(np.asarray(inputs["L0_attn_W_o"], np.float32).T), 8),
        "w_q": _pack_chunks(np.asarray(inputs["L0_ffn_Wq"], np.float32), 8),
        "w_rt": _pack_chunks(np.ascontiguousarray(np.asarray(inputs["L0_ffn_routers"], np.float32).T), 8),
        "w_lvt": _pack_chunks(np.ascontiguousarray(np.asarray(inputs["L0_ffn_leaf_values"], np.float32).T), 8),
        "w_outt": _pack_chunks(np.ascontiguousarray(np.asarray(inputs["L0_ffn_Wout_w"], np.float32).T), 8),
        "w_outb": np.asarray(inputs["L0_ffn_Wout_b"], np.float32).reshape(1, D),
        "identb": np.eye(128, dtype=np.float32),
        "onesb": np.ones((1, 128), np.float32),
    }
    qg = np.asarray(inputs["L0_attn_q_ln_g"], np.float32).reshape(2, 128).T
    qb = np.asarray(inputs["L0_attn_q_ln_b"], np.float32).reshape(2, 128).T
    kg = np.asarray(inputs["L0_attn_kv_ln_g"], np.float32).reshape(2, 128).T
    kb = np.asarray(inputs["L0_attn_kv_ln_b"], np.float32).reshape(2, 128).T
    common["vecs"] = np.ascontiguousarray(
        np.concatenate([qg, qb, kg, kb], axis=1))

    maps = []
    for core in range(8):
        b, par = core // 2, core % 2
        xs = x[b].reshape(NT, 128, D)
        if par == 1:
            perm = np.arange(NT).reshape(-1, 2)[:, ::-1].reshape(-1)
            xs = xs[perm]
        odd = np.ones((128, 128), np.float32) if par == 1 else np.zeros((128, 128), np.float32)
        m4 = np.ascontiguousarray(np.concatenate([tri, odd, tri, odd], axis=1))
        mm = dict(common)
        mm["xseq"] = np.ascontiguousarray(xs.reshape(S, D))
        mm["mask4"] = m4
        maps.append(mm)
    return maps


def kernel(**inputs):
    nc = build_program(debug=False)
    maps = make_in_maps(inputs)
    res = run_bass_kernel_spmd(nc, maps, core_ids=list(range(8)))
    out = np.empty((4, S, D), dtype=np.float32)
    for core in range(8):
        b, par = core // 2, core % 2
        o = res.results[core]["out"].reshape(OWN, 128, D)
        out.reshape(4, NT, 128, D)[b, par::2] = o
    return out


# version 19
# speedup vs baseline: 1.0384x; 1.0384x over previous
import numpy as np
import concourse.bass as bass
import concourse.mybir as mybir
from concourse.bass_utils import run_bass_kernel_spmd

F32 = mybir.dt.float32
BF16 = mybir.dt.bfloat16
AF = mybir.ActivationFunctionType
ALU = mybir.AluOpType
AX = mybir.AxisListType

D = 1024
S = 4096
NT = 32
OWN = 16
CP = 256
EPS = 1e-5
ENGINES = ("pe", "act", "dve", "sp")
SELF_SYNC = ("dve", "act")
LAST_SS = None

SLOT = {"ln1_0": 0, "ln1_1": 6, "kv_0": 12, "kv_1": 18, "cq_0": 24, "cq_1": 30,
        "attn_0": 36, "attn_1": 42, "ln2_0": 48, "ln2_1": 54, "ffn_0": 60,
        "ffn_1": 66, "attn_2": 72, "ln2_2": 78, "ffn_2": 84, "ln1_2": 90,
        "kv_2": 96}


class Sched:
    def __init__(self):
        self.ops = []

    def add(self, engine, fn, reads=(), writes=(), dbg="", cost=None):
        self.ops.append(dict(engine=engine, fn=fn, reads=tuple(reads),
                             writes=tuple(writes), dma=None, dbg=dbg,
                             cost=cost))

    def dma(self, queue, fn, reads=(), writes=(), sem=None, dbg="", cost=None):
        assert sem is not None
        self.ops.append(dict(engine=queue, fn=fn, reads=tuple(reads),
                             writes=tuple(writes), dma="dq_" + sem, dbg=dbg,
                             cost=cost))

    def barrier(self):
        self.ops.append(dict(engine=None, fn=None, reads=(), writes=(),
                             dma=None, dbg="barrier", barrier=True))

    def reorder(self):
        segs = []
        cur = []
        for op in self.ops:
            if op.get("barrier"):
                segs.append(cur)
                segs.append([op])
                cur = []
            else:
                cur.append(op)
        segs.append(cur)
        out = []
        for seg in segs:
            if len(seg) == 1 and seg[0].get("barrier"):
                out.extend(seg)
            else:
                out.extend(self._sched_seg(seg))
        self.ops = out

    def _sched_seg(self, seg):
        import heapq
        n = len(seg)
        if n <= 2:
            return seg
        preds = [set() for _ in range(n)]
        succs = [[] for _ in range(n)]
        led = {}
        lastq = {}
        for i, op in enumerate(seg):
            ps_r = tuple(k for k in op["reads"] if k.startswith("ps")
                         and k not in op["writes"])
            eff_w = op["writes"] + ps_r
            eff_r = tuple(k for k in op["reads"] if not k.startswith("ps"))
            dep = set()
            for r in eff_r:
                e = led.get(r)
                if e and e["w"] is not None:
                    dep.add(e["w"])
            for w in eff_w:
                e = led.get(w)
                if e:
                    if e["w"] is not None:
                        dep.add(e["w"])
                    dep.update(e["r"])
            if op["dma"]:
                q = op["dma"]
                if q in lastq:
                    dep.add(lastq[q])
                lastq[q] = i
            for w in eff_w:
                led[w] = {"w": i, "r": []}
            for r in eff_r:
                led.setdefault(r, {"w": None, "r": []})["r"].append(i)
            dep.discard(i)
            preds[i] = dep
            for d in dep:
                succs[d].append(i)
        cost = [op.get("cost") or (1500.0 if op["dma"] else 250.0)
                for op in seg]
        prio = [0.0] * n
        for i in range(n - 1, -1, -1):
            m = 0.0
            for s in succs[i]:
                if prio[s] > m:
                    m = prio[s]
            prio[i] = m + cost[i]
        npred = [len(p) for p in preds]
        engines = sorted(set(op["engine"] for op in seg))
        avail = {e: [] for e in engines}
        busy = {e: False for e in engines}
        order = []
        ev = []
        seq = 0
        for i in range(n):
            if npred[i] == 0:
                heapq.heappush(avail[seg[i]["engine"]], (-prio[i], i))

        def dispatch(e, now):
            nonlocal seq
            if busy[e] or not avail[e]:
                return
            _, i = heapq.heappop(avail[e])
            busy[e] = True
            order.append(i)
            if seg[i]["dma"]:
                occ, comp = 60.0, 60.0 + cost[i]
            else:
                occ = comp = cost[i]
            seq += 1
            heapq.heappush(ev, (now + occ, seq, "free", e))
            seq += 1
            heapq.heappush(ev, (now + comp, seq, "done", i))

        for e in engines:
            dispatch(e, 0.0)
        while len(order) < n:
            t, _, kind, pl = heapq.heappop(ev)
            if kind == "free":
                busy[pl] = False
                dispatch(pl, t)
            else:
                for s in succs[pl]:
                    npred[s] -= 1
                    if npred[s] == 0:
                        es = seg[s]["engine"]
                        heapq.heappush(avail[es], (-prio[s], s))
                        dispatch(es, t)
        return [seg[i] for i in order]

    def resolve(self):
        pos = {e: 0 for e in ENGINES}
        last_compute = {e: -1 for e in ENGINES}
        dmacnt = {}
        ledger = {}
        vec = {e: {} for e in ENGINES}
        hist = {e: [] for e in ENGINES}
        self.wait_targets = {e: set() for e in ENGINES}

        def need(t, eng):
            if t[0] == "eng":
                _, f, p = t
                if f == eng and eng not in SELF_SYNC:
                    return None
                if vec[eng].get(f, -1) >= p:
                    return None
            else:
                _, q, c = t
                if vec[eng].get(q, 0) >= c:
                    return None
            return t

        def apply_wait(eng, t):
            if t[0] == "eng":
                _, f, p = t
                self.wait_targets[f].add(p)
                if vec[eng].get(f, -1) < p:
                    vec[eng][f] = p
                    h = hist[f]
                    if p < len(h):
                        for k, v in h[p].items():
                            if vec[eng].get(k, -1 if isinstance(v, int) else 0) < v:
                                vec[eng][k] = v
            else:
                _, q, c = t
                if vec[eng].get(q, 0) < c:
                    vec[eng][q] = c

        for op in self.ops:
            if op.get("barrier"):
                op["bwaits"] = {}
                for e in ENGINES:
                    w = []
                    for f in ENGINES:
                        if last_compute[f] >= 0 and (f != e or e in SELF_SYNC):
                            t = need(("eng", f, last_compute[f]), e)
                            if t:
                                w.append(t)
                    for q, c in dmacnt.items():
                        if c > 0:
                            t = need(("dma", q, c), e)
                            if t:
                                w.append(t)
                    op["bwaits"][e] = w
                    for t in w:
                        apply_wait(e, t)
                continue
            eng = op["engine"]
            ps_reads = tuple(k for k in op["reads"] if k.startswith("ps")
                             and k not in op["writes"])
            eff_writes = op["writes"] + ps_reads
            eff_reads = tuple(k for k in op["reads"] if not k.startswith("ps"))
            waits = {}
            for r in eff_reads:
                ent = ledger.get(r)
                if ent and ent["w"] is not None:
                    t = need(ent["w"], eng)
                    if t:
                        k = t[1]
                        if k not in waits or waits[k][2] < t[2]:
                            waits[k] = t
            for wkey in eff_writes:
                ent = ledger.get(wkey)
                if ent:
                    cands = ([ent["w"]] if ent["w"] is not None else []) + ent["r"]
                    for c in cands:
                        t = need(c, eng)
                        if t:
                            k = t[1]
                            if k not in waits or waits[k][2] < t[2]:
                                waits[k] = t
            op["waits"] = list(waits.values())
            for t in op["waits"]:
                apply_wait(eng, t)
            p = pos[eng]
            if op["dma"]:
                q = op["dma"]
                dmacnt[q] = dmacnt.get(q, 0) + 16
                mytok = ("dma", q, dmacnt[q])
            else:
                mytok = ("eng", eng, p)
                last_compute[eng] = p
            for wkey in eff_writes:
                ledger[wkey] = {"w": mytok, "r": []}
            for r in eff_reads:
                ledger.setdefault(r, {"w": None, "r": []})["r"].append(mytok)
            op["pos"] = p
            hist[eng].append(dict(vec[eng]))
            pos[eng] += 1
        self.final_dma = dict(dmacnt)
        self.flagged = {e: sorted(self.wait_targets[e]) for e in ENGINES}

    def waitval(self, f, p):
        import bisect
        return bisect.bisect_right(self.flagged[f], p)


def _pack_chunks(w, c):
    n = w.shape[1]
    return np.ascontiguousarray(w.reshape(c, 128, n).transpose(1, 0, 2))


def build_program(debug=False, phases='ABC'):
    nc = bass.Bass("TRN2", target_bir_lowering=False)

    xd = nc.dram_tensor("xseq", [S, D], F32, kind="ExternalInput").ap()
    w_dq = nc.dram_tensor("w_dq", [128, 8, CP], F32, kind="ExternalInput").ap()
    w_uq = nc.dram_tensor("w_uq", [128, 2, D], F32, kind="ExternalInput").ap()
    w_dkv = nc.dram_tensor("w_dkv", [128, 8, CP], F32, kind="ExternalInput").ap()
    w_ukv = nc.dram_tensor("w_ukv", [128, 2, 2 * D], F32, kind="ExternalInput").ap()
    w_ot = nc.dram_tensor("w_ot", [128, 8, D], F32, kind="ExternalInput").ap()
    w_q = nc.dram_tensor("w_q", [128, 8, D], F32, kind="ExternalInput").ap()
    w_rt = nc.dram_tensor("w_rt", [128, 8, 255], F32, kind="ExternalInput").ap()
    w_lvt = nc.dram_tensor("w_lvt", [128, 8, 256], F32, kind="ExternalInput").ap()
    w_outt = nc.dram_tensor("w_outt", [128, 8, D], F32, kind="ExternalInput").ap()
    w_outb = nc.dram_tensor("w_outb", [1, D], F32, kind="ExternalInput").ap()
    vecs = nc.dram_tensor("vecs", [128, 8], F32, kind="ExternalInput").ap()
    mask4 = nc.dram_tensor("mask4", [128, 512], F32, kind="ExternalInput").ap()
    identb = nc.dram_tensor("identb", [128, 128], F32, kind="ExternalInput").ap()
    onesb = nc.dram_tensor("onesb", [1, 128], F32, kind="ExternalInput").ap()
    outd = nc.dram_tensor("out", [OWN * 128, D], F32, kind="ExternalOutput").ap()
    dbg_t = {}
    if debug:
        dbg_t["ckvf"] = nc.dram_tensor("dbg_ckvf", [128, 2 * S], BF16, kind="ExternalOutput").ap()
        dbg_t["cqf"] = nc.dram_tensor("dbg_cqf", [128, 2 * 2048], BF16, kind="ExternalOutput").ap()
        dbg_t["kpair"] = nc.dram_tensor("dbg_kpair", [128, S], BF16, kind="ExternalOutput").ap()
        dbg_t["vpair"] = nc.dram_tensor("dbg_vpair", [128, NT * 260], BF16, kind="ExternalOutput").ap()
        dbg_t["qpair"] = nc.dram_tensor("dbg_qpair", [128, 2048], BF16, kind="ExternalOutput").ap()
        dbg_t["oall"] = nc.dram_tensor("dbg_oall", [128, 8 * 2048], BF16, kind="ExternalOutput").ap()

    SS = Sched()
    global LAST_SS
    LAST_SS = SS
    alive = {}

    def sb(name, shape, dtype):
        g = nc.sbuf_tensor(name, shape, dtype)
        h = g.__enter__()
        alive[name] = g
        return h

    def psum(name, shape, dtype):
        g = nc.psum_tensor(name, shape, dtype)
        h = g.__enter__()
        alive[name] = g
        return h

    def free(*names):
        for n in reversed(names):
            alive.pop(n).__exit__(None, None, None)

    ident_b = sb("ident_b", [128, 128], BF16)
    mask_sb = sb("mask_sb", [128, 512], BF16)
    ones_b = sb("ones_b", [1, 128], BF16)
    vec_sb = sb("vec_sb", [128, 8], F32)
    woutb_b = sb("woutb_b", [1, D], BF16)
    stat = sb("stat", [128, 102], F32)
    wuq = sb("wuq", [128, 2 * D], BF16)
    wukv = sb("wukv", [128, 2 * 2 * D], BF16)
    wot = sb("wot", [128, 8 * D], BF16)
    wgT = sb("wgT", [128, 8 * 255], BF16)
    w2 = sb("w2", [128, 2 * D], BF16)
    o_all = sb("o_all", [128, 8 * 2048], BF16)
    ckvf = sb("ckvf", [128, 2 * S], BF16)
    cqf = sb("cqf", [128, 2 * 2048], BF16)
    stage = [sb("stage0", [128, 1024], F32), sb("stage1", [128, 1024], F32)]
    stageg = [sb("stageg0", [128, 1024], F32), sb("stageg1", [128, 1024], F32)]

    def wload(dst_sb, src_ap, total, tag, queue="sp"):
        stgs = stageg if queue == "gp" else stage
        pfx = "stgg" if queue == "gp" else "stg"
        if queue == "gp":
            queue = "sp"
        chunks = [(i, min(1024, total - i)) for i in range(0, total, 1024)]
        for idx, (off, ln) in enumerate(chunks):
            stg = stgs[idx % 2]
            stgk = f"{pfx}{idx % 2}"
            src = src_ap.rearrange("p a b -> p (a b)")[:, off:off + ln]

            def dfn(e, stg=stg, src=src, ln=ln):
                return e.dma_start(stg[:, 0:ln], src)

            def cfn(e, stg=stg, dst=dst_sb, off=off, ln=ln):
                return e.tensor_copy(dst[:, off:off + ln], stg[:, 0:ln])

            SS.dma(queue, dfn, reads=(stgk,), writes=(stgk,), sem=stgk,
                   dbg=f"ld_{tag}{idx}", cost=ln * 1.3 + 1000)
            SS.add("dve", cfn, reads=(stgk,), writes=(tag,), dbg=f"cast_{tag}{idx}",
                   cost=ln * 0.7 + 100)

    xre = xd.rearrange("(t p) d -> p t d", p=128)

    SS.dma("sp", lambda e: e.dma_start(stage[0][:, 0:128], identb),
           reads=("stg0",), writes=("stg0",), sem="stg0")
    SS.add("dve", lambda e: e.tensor_copy(ident_b[:], stage[0][:, 0:128]),
           reads=("stg0",), writes=("ident_b",))
    SS.dma("sp", lambda e: e.dma_start(stage[1][:, 0:512], mask4),
           reads=("stg1",), writes=("stg1",), sem="stg1")
    SS.add("dve", lambda e: e.tensor_copy(mask_sb[:], stage[1][:, 0:512]),
           reads=("stg1",), writes=("mask_sb",))
    SS.dma("sp", lambda e: e.dma_start(stage[0][0:1, 128:256], onesb),
           reads=("stg0",), writes=("stg0",), sem="stg0")
    SS.add("dve", lambda e: e.tensor_copy(ones_b[:], stage[0][0:1, 128:256]),
           reads=("stg0",), writes=("ones_b",))
    SS.dma("sp", lambda e: e.dma_start(vec_sb[:], vecs), writes=("vec_sb",),
           sem="once")
    SS.dma("sp", lambda e: e.dma_start(woutb_f[:], w_outb), writes=("woutb_f",),
           sem="once")
    SS.add("dve", lambda e: e.tensor_copy(woutb_b[:], woutb_f[:]),
           reads=("woutb_f",), writes=("woutb_b",))

    wdq = sb("wdq", [128, 8 * CP], BF16)
    wdkv = sb("wdkv", [128, 8 * CP], BF16)
    wq = sb("wq", [128, 8 * D], BF16)
    wrt = sb("wrt", [128, 8 * 255], BF16)
    wlvt = sb("wlvt", [128, 8 * 256], BF16)
    woutt = sb("woutt", [128, 8 * D], BF16)
    woutb_f = sb("woutb_f", [1, D], F32)
    xstg = [sb("xstg0", [128, 2 * D], F32), sb("xstg1", [128, 2 * D], F32)]
    sqs = [sb(f"sqs{i}", [128, D], F32) for i in range(3)]
    xnb = [sb(f"xnb{i}", [128, D], BF16) for i in range(3)]
    xnf = [sb(f"xnf{i}", [128, D], BF16) for i in range(3)]
    ps_tr = [psum(f"ps_tr{i}", [128, 1024], BF16) for i in range(3)]
    ps_ckv = [psum(f"ps_ckv{i}", [128, 512], F32) for i in range(3)]
    ps_cq = psum("ps_cq", [128, 512], F32)
    ps_pw = psum("ps_pw", [128, 512], F32)

    SS.dma("sp", lambda e: e.dma_start(
        xstg[0][:].rearrange("p (t d) -> p t d", t=2), xre[:, 0:2, :]),
        reads=("xstg_0",), writes=("xstg_0",), sem="xstg_0", dbg="ldx_0")
    wload(wdkv, w_dkv, 8 * CP, "wdkv", queue="gp")
    wload(wukv, w_ukv, 2 * 2 * D, "wukv", queue="gp")
    wload(wdq, w_dq, 8 * CP, "wdq", queue="gp")
    wload(wuq, w_uq, 2 * D, "wuq", queue="gp")

    def ln_stats(tag, slot, src_key, src_ap_fn, fd, sq_buf, sq_key,
                 extra_reads=()):
        c0 = SLOT[slot]
        k = slot + "_"

        def f_sum(e, src_ap_fn=src_ap_fn, c0=c0):
            return e.reduce_sum(stat[:, c0:c0 + 1], src_ap_fn(), AX.X)

        def f_sq(e, src_ap_fn=src_ap_fn, c0=c0, sq_buf=sq_buf, fd=fd):
            return e.activation(sq_buf[:, 0:fd], src_ap_fn(), AF.Square,
                                accum_out=stat[:, c0 + 1:c0 + 2])

        inv_n = 1.0 / fd

        def f_mean(e, c0=c0, inv_n=inv_n):
            return e.tensor_scalar(stat[:, c0 + 2:c0 + 3], stat[:, c0:c0 + 1],
                                   inv_n, None, ALU.mult)

        def f_msq(e, c0=c0):
            return e.tensor_scalar(stat[:, c0 + 3:c0 + 4], stat[:, c0 + 2:c0 + 3],
                                   stat[:, c0 + 2:c0 + 3], EPS,
                                   ALU.mult, ALU.subtract)

        def f_var(e, c0=c0, inv_n=inv_n):
            return e.scalar_tensor_tensor(stat[:, c0 + 3:c0 + 4],
                                          stat[:, c0 + 1:c0 + 2], inv_n,
                                          stat[:, c0 + 3:c0 + 4],
                                          ALU.mult, ALU.subtract)

        def f_ln(e, c0=c0):
            return e.activation(stat[:, c0 + 4:c0 + 5], stat[:, c0 + 3:c0 + 4],
                                AF.Ln)

        def f_ex(e, c0=c0):
            return e.activation(stat[:, c0 + 4:c0 + 5], stat[:, c0 + 4:c0 + 5],
                                AF.Exp, scale=-0.5)

        def f_nmr(e, c0=c0):
            return e.tensor_scalar(stat[:, c0 + 5:c0 + 6], stat[:, c0 + 2:c0 + 3],
                                   stat[:, c0 + 4:c0 + 5], -1.0, ALU.mult, ALU.mult)

        rd = (src_key,) + tuple(extra_reads)
        ops = [("dve", f_sum, rd, (k + "s",), f"{tag}_sum", fd * 1.05 + 150),
               ("act", f_sq, rd, (sq_key, k + "q"), f"{tag}_sq", fd * 0.85 + 300),
               ("dve", f_mean, (k + "s",), (k + "m",), f"{tag}_mean", 80),
               ("dve", f_msq, (k + "m",), (k + "v",), f"{tag}_msq", 80),
               ("dve", f_var, (k + "q", k + "v"), (k + "v",), f"{tag}_var", 80),
               ("act", f_ln, (k + "v",), (k + "r",), f"{tag}_ln", 300),
               ("act", f_ex, (k + "r",), (k + "r",), f"{tag}_exp", 300),
               ("dve", f_nmr, (k + "m", k + "r"), (k + "nmr",), f"{tag}_nmr", 80)]
        for eng_, fn_, rd_, wr_, dbg_, c_ in ops:
            SS.add(eng_, fn_, reads=rd_, writes=wr_, dbg=dbg_, cost=c_)
        return (stat[:, c0 + 4:c0 + 5], stat[:, c0 + 5:c0 + 6], k)

    nt_emit = NT
    lvl = 99
    if phases.startswith("A") and len(phases) > 1:
        parts = phases[1:].split("x")
        if parts[0].isdigit():
            nt_emit = int(parts[0])
        if len(parts) > 1 and parts[1].isdigit():
            lvl = int(parts[1])
    for t in range(nt_emit):
        par = t % 3
        j = t // 2
        if t % 2 == 0 and t > 0:
            def fx(e, j=j, b=xstg[j % 2]):
                return e.dma_start(b[:].rearrange("p (t d) -> p t d", t=2),
                                   xre[:, 2 * j:2 * (j + 1), :])
            SS.dma("sp", fx, reads=(f"xstg_{j % 2}",), writes=(f"xstg_{j % 2}",),
                   sem=f"xstg_{j % 2}", dbg=f"ldx_{j}", cost=3300)
        xbuf = xstg[j % 2]
        xk = f"xstg_{j % 2}"

        def xview(xbuf=xbuf, t=t):
            return xbuf[:, D * (t % 2):D * (t % 2 + 1)]

        if lvl < 1:
            continue
        sc, bi, k1 = ln_stats(f"ln1_{t}", f"ln1_{par}", xk, xview, D,
                              sqs[par], f"sqs{par}")

        def f_ap(e, xview=xview, o=xnb[par], sc=sc, bi=bi):
            return e.activation(o[:], xview(), AF.Identity, bias=bi, scale=sc)
        if lvl < 2:
            continue
        SS.add("act", f_ap, reads=(xk, k1 + "r", k1 + "nmr"),
               writes=(f"xnb{par}",), dbg=f"ln1ap_{t}", cost=1150)

        if lvl < 3:
            continue
        for c in range(8):
            def f_tr(e, c=c, i=xnb[par], o=ps_tr[par]):
                return e.transpose(o[:, 128 * c:128 * (c + 1)],
                                   i[:, 128 * c:128 * (c + 1)], ident_b[:])
            SS.add("pe", f_tr, reads=(f"xnb{par}", "ident_b"),
                   writes=(f"pstr{par}",), dbg=f"xnT_{t}_{c}", cost=320)

        def f_ev(e, i=ps_tr[par], o=xnf[par]):
            return e.tensor_copy(o[:], i[:])
        SS.add("dve", f_ev, reads=(f"pstr{par}",), writes=(f"xnf{par}",),
               dbg=f"xnTe_{t}", cost=850)
        if lvl < 4:
            continue

        for c in range(8):
            def f_mm(e, c=c, i=xnf[par], o=ps_ckv[par]):
                return e.matmul(o[:, 0:CP], i[:, 128 * c:128 * (c + 1)],
                                wdkv[:, CP * c:CP * (c + 1)],
                                start=(c == 0), stop=(c == 7))
            SS.add("pe", f_mm, reads=(f"xnf{par}", "wdkv"),
                   writes=(f"psckv{par}",), dbg=f"ckvmm_{t}_{c}", cost=190)

        if lvl < 5:
            continue
        sck, bik, k2 = ln_stats(f"kv_{t}", f"kv_{par}", f"psckv{par}",
                                (lambda o=ps_ckv[par]: o[:, 0:CP]), CP,
                                sqs[par], f"sqs{par}")

        def f_apk(e, o=xnb[par], i=ps_ckv[par], sck=sck, bik=bik):
            return e.activation(o[:, 0:CP], i[:, 0:CP], AF.Identity,
                                bias=bik, scale=sck)
        if lvl < 6:
            continue
        SS.add("act", f_apk, reads=(f"psckv{par}", k2 + "r", k2 + "nmr"),
               writes=(f"xnb{par}",), dbg=f"kvap_{t}", cost=500)
        if lvl < 7:
            continue
        for c in range(2):
            def f_trk(e, c=c, i=xnb[par], o=ps_tr[par]):
                return e.transpose(o[:, 128 * c:128 * (c + 1)],
                                   i[:, 128 * c:128 * (c + 1)], ident_b[:])
            SS.add("pe", f_trk, reads=(f"xnb{par}", "ident_b"),
                   writes=(f"pstr{par}",), dbg=f"ckvT_{t}_{c}", cost=320)
        for c in range(2):
            def f_evk(e, c=c, i=ps_tr[par], t=t):
                return e.tensor_scalar(ckvf[:, S * c + 128 * t:S * c + 128 * (t + 1)],
                                       i[:, 128 * c:128 * (c + 1)],
                                       vec_sb[:, 4 + c:5 + c], vec_sb[:, 6 + c:7 + c],
                                       ALU.mult, ALU.add)
            SS.add("dve", f_evk, reads=(f"pstr{par}", "vec_sb"),
                   writes=(f"ckvf{t // 4}",), dbg=f"ckvev_{t}_{c}", cost=280)

        if lvl < 8:
            continue
        if t % 2 == 0:
            i_own = t // 2
            for c in range(8):
                def f_mmq(e, c=c, i=xnf[par]):
                    return e.matmul(ps_cq[:, 0:CP], i[:, 128 * c:128 * (c + 1)],
                                    wdq[:, CP * c:CP * (c + 1)],
                                    start=(c == 0), stop=(c == 7))
                SS.add("pe", f_mmq, reads=(f"xnf{par}", "wdq"),
                       writes=("pscq",), dbg=f"cqmm_{t}_{c}", cost=190)
            scq, biq, k3 = ln_stats(f"cq_{t}", f"cq_{(t // 2) % 2}", "pscq",
                                    (lambda: ps_cq[:, 0:CP]), CP,
                                    sqs[par], f"sqs{par}")

            def f_apq(e, o=xnb[par], scq=scq, biq=biq):
                return e.activation(o[:, 0:CP], ps_cq[:, 0:CP], AF.Identity,
                                    bias=biq, scale=scq)
            SS.add("act", f_apq, reads=("pscq", k3 + "r", k3 + "nmr"),
                   writes=(f"xnb{par}",), dbg=f"cqap_{t}", cost=500)
            for c in range(2):
                def f_trq(e, c=c, i=xnb[par], o=ps_tr[par]):
                    return e.transpose(o[:, 128 * c:128 * (c + 1)],
                                       i[:, 128 * c:128 * (c + 1)], ident_b[:])
                SS.add("pe", f_trq, reads=(f"xnb{par}", "ident_b"),
                       writes=(f"pstr{par}",), dbg=f"cqT_{t}_{c}", cost=320)
            for c in range(2):
                def f_evq(e, c=c, i=ps_tr[par], i_own=i_own):
                    return e.tensor_scalar(
                        cqf[:, 2048 * c + 128 * i_own:2048 * c + 128 * (i_own + 1)],
                        i[:, 128 * c:128 * (c + 1)],
                        vec_sb[:, 0 + c:1 + c], vec_sb[:, 2 + c:3 + c],
                        ALU.mult, ALU.add)
                SS.add("dve", f_evq, reads=(f"pstr{par}", "vec_sb"),
                       writes=(f"cqf{i_own // 4}",), dbg=f"cqev_{t}_{c}", cost=280)

    wload(wq, w_q, 8 * D, "wq")
    wload(wrt, w_rt, 8 * 255, "wrt")
    wload(wlvt, w_lvt, 8 * 256, "wlvt")
    wload(woutt, w_outt, 8 * D, "woutt")
    for dsub in range(8):
        for ec in range(8):
            def f_pg(e, dsub=dsub, ec=ec):
                return e.matmul(ps_pw[:, 0:255],
                                wq[:, D * ec + 128 * dsub:D * ec + 128 * (dsub + 1)],
                                wrt[:, 255 * ec:255 * (ec + 1)],
                                start=(ec == 0), stop=(ec == 7))
            SS.add("pe", f_pg, reads=("wq", "wrt"), writes=("pspw",),
                   dbg=f"pg_{dsub}_{ec}", cost=190)

        def f_pge(e, dsub=dsub):
            return e.tensor_copy(wgT[:, 255 * dsub:255 * (dsub + 1)], ps_pw[:, 0:255])
        SS.add("dve", f_pge, reads=("pspw",), writes=("wgT",), dbg=f"pge_{dsub}",
               cost=370)
    for lc in range(2):
        for h in range(2):
            for dc in range(8):
                def f_p2(e, lc=lc, h=h, dc=dc):
                    return e.matmul(ps_pw[:, 0:512],
                                    wlvt[:, 256 * dc + 128 * lc:256 * dc + 128 * (lc + 1)],
                                    woutt[:, D * dc + 512 * h:D * dc + 512 * (h + 1)],
                                    start=(dc == 0), stop=(dc == 7))
                SS.add("pe", f_p2, reads=("wlvt", "woutt"), writes=("pspw",),
                       dbg=f"p2_{lc}_{h}_{dc}", cost=300)

            def f_p2e(e, lc=lc, h=h):
                return e.tensor_copy(w2[:, D * lc + 512 * h:D * lc + 512 * (h + 1)],
                                     ps_pw[:, 0:512])
            SS.add("dve", f_p2e, reads=("pspw",), writes=("w2",), dbg=f"p2e_{lc}_{h}",
                   cost=460)

    SS.barrier()
    free("ps_tr0", "ps_tr1", "ps_tr2", "ps_ckv0", "ps_ckv1", "ps_ckv2",
         "ps_cq", "ps_pw")
    free("wdq", "wdkv", "wq", "wrt", "wlvt", "woutt", "woutb_f",
         "xstg0", "xstg1", "sqs0", "sqs1", "sqs2", "xnb0", "xnb1", "xnb2",
         "xnf0", "xnf1", "xnf2")

    kp = [sb("kp0", [128, S], BF16), sb("kp1", [128, S], BF16)]
    vpg = [sb("vpg0", [128, NT * 260], BF16), sb("vpg1", [128, NT * 260], BF16)]
    qp = [sb("qp0", [128, 2048], BF16), sb("qp1", [128, 2048], BF16)]
    pb = [sb(f"pb{i}", [128, 1024], BF16) for i in range(4)]
    d_t = sb("d_t", [32, 1024], F32)
    d_ti = sb("d_ti", [32, 1024], F32)
    d_bc = sb("d_bc", [64, 1024], F32)
    ps_s = [psum("ps_s0", [128, 1024], F32), psum("ps_s1", [128, 1024], F32)]
    ps_av = psum("ps_av", [128, 1024], F32)
    ps_j = [psum("ps_j0", [128, 512], F32), psum("ps_j1", [128, 512], F32)]

    for b in range(2):
        def f_ones(e, b=b):
            return e.memset(vpg[b][:].rearrange("p (t q h c) -> p t q h c",
                                                t=NT, q=2, h=2)[:, :, :, :, 64:65], 1.0)
        SS.add("dve", f_ones, reads=(),
               writes=tuple(f"vpg{b}_{tg}" for tg in range(16)),
               dbg=f"vones_{b}")

    zrow = sb("zrow", [1, 512], BF16)
    SS.add("dve", lambda e: e.memset(zrow[:], 0.0), writes=("zrow",), dbg="zrow")
    SS.add("dve", lambda e: e.memset(ps_av[64:96, :], 0.0), writes=("psav",),
           dbg="avinit")
    SS.add("dve", lambda e: e.memset(d_ti[0:32, :], 0.0), writes=("d_ti",), dbg="dti0")

    wload(wot, w_ot, 8 * D, "wot")

    def jit_ops(p):
        b = p % 2
        ops = []
        for kc in range(8):
            for c in range(2):
                def f(e, p=p, b=b, kc=kc, c=c):
                    return e.matmul(ps_j[b][:, 0:512],
                                    wukv[:, 2 * D * c + 128 * p:2 * D * c + 128 * (p + 1)],
                                    ckvf[:, S * c + 512 * kc:S * c + 512 * (kc + 1)],
                                    start=(c == 0), stop=(c == 1))
                ops.append(("pe", f, ("wukv", f"ckvf{kc}"), (f"psj{b}",), f"kj_{p}_{kc}_{c}", 300))

            def fe(e, b=b, kc=kc):
                return e.tensor_copy(kp[b][:, 512 * kc:512 * (kc + 1)], ps_j[b][:, 0:512])
            ops.append(("dve", fe, (f"psj{b}",), (f"kp{b}_{kc}",), f"kje_{p}_{kc}", 460))
        for qc in range(4):
            for c in range(2):
                def f(e, p=p, b=b, qc=qc, c=c):
                    return e.matmul(ps_j[b][:, 0:512],
                                    wuq[:, D * c + 128 * p:D * c + 128 * (p + 1)],
                                    cqf[:, 2048 * c + 512 * qc:2048 * c + 512 * (qc + 1)],
                                    start=(c == 0), stop=(c == 1))
                ops.append(("pe", f, ("wuq", f"cqf{qc}"), (f"psj{b}",), f"qj_{p}_{qc}_{c}", 300))

            def fe(e, b=b, qc=qc):
                return e.tensor_scalar(qp[b][:, 512 * qc:512 * (qc + 1)],
                                       ps_j[b][:, 0:512], 0.125, None, ALU.mult)
            ops.append(("dve", fe, (f"psj{b}",), (f"qp{b}_{qc}",), f"qje_{p}_{qc}", 460))
        return ops

    def emit_jit_v(g):
        b = g % 2
        for tg in range(16):
            for tt in range(2):
                for c in range(2):
                    def f(e, g=g, b=b, tg=tg, tt=tt, c=c):
                        t_ = 2 * tg + tt
                        return e.matmul(
                            ps_j[b][:, 256 * tt:256 * (tt + 1)],
                            ckvf[:, S * c + 128 * t_:S * c + 128 * (t_ + 1)],
                            wukv[:, 2 * D * c + D + 256 * g:2 * D * c + D + 256 * (g + 1)],
                            start=(c == 0), stop=(c == 1))
                    SS.add("pe", f, reads=(f"ckvf{(2 * tg + tt) // 4}", "wukv"),
                           writes=(f"psj{b}",),
                           dbg=f"vj_{g}_{tg}_{tt}_{c}", cost=190)

            def fe(e, b=b, g=g, tg=tg):
                src = ps_j[b][:, 0:512].rearrange("p (t q h c) -> p t q h c",
                                                  t=2, q=2, h=2)
                dst = vpg[g % 2][:].rearrange("p (t q h c) -> p t q h c",
                                              t=NT, q=2, h=2)[:, 2 * tg:2 * (tg + 1), :, :, 0:64]
                return e.tensor_copy(dst, src)
            SS.add("dve", fe, reads=(f"psj{b}",), writes=(f"vpg{g % 2}_{tg}",),
                   dbg=f"vje_{g}_{tg}", cost=560)

    pending_jit = jit_ops(0) if 'B' in phases else []
    for o in pending_jit:
        SS.add(*o[0:2], reads=o[2], writes=o[3], dbg=o[4], cost=o[5])
    if 'B' in phases:
        emit_jit_v(0)
    pending_jit = jit_ops(1) if 'B' in phases else []

    avs = sb("avs", [128, 1024], F32)

    for p in (range(8) if 'B' in phases else []):
        b = p % 2
        for c in range(4):
            nkt = 8 * c + 8

            def emit_sc(kt, b=b, c=c, p=p):
                m = kt - 8 * c
                colstart = 128 * (m // 2) if m >= 0 else 0
                sp_ = ps_s[kt % 2]

                def f_sc(e, b=b, c=c, kt=kt, colstart=colstart, sp_=sp_):
                    for h2 in range(2):
                        ins = e.matmul(
                            sp_[:, 512 * h2 + colstart:512 * (h2 + 1)],
                            kp[b][64 * h2:64 * (h2 + 1), 128 * kt:128 * (kt + 1)],
                            qp[b][64 * h2:64 * (h2 + 1),
                                  512 * c + colstart:512 * (c + 1)],
                            start=True, stop=True, tile_position=(64 * h2, 0))
                    return ins
                SS.add("pe", f_sc, reads=(f"kp{b}_{kt // 4}", f"qp{b}_{c}"),
                       writes=(f"pss{kt % 2}",), dbg=f"sc_{p}_{c}_{kt}",
                       cost=(512 - colstart) * 0.45 + 170)

            def emit_tail(kt, b=b, c=c, p=p):
                m = kt - 8 * c
                colstart = 128 * (m // 2) if m >= 0 else 0
                sp_ = ps_s[kt % 2]
                pbx = pb[kt % 4]

                def f_ex(e, sp_=sp_, pbx=pbx, colstart=colstart):
                    iv = sp_[:].rearrange("p (h q) -> p h q", h=2)[:, :, colstart:512]
                    ov = pbx[:].rearrange("p (h q) -> p h q", h=2)[:, :, colstart:512]
                    return e.activation(ov, iv, AF.Exp)
                SS.add("act", f_ex, reads=(f"pss{kt % 2}",),
                       writes=(f"pb{kt % 4}",), dbg=f"exp_{p}_{c}_{kt}",
                       cost=(2 * (512 - colstart) + 352) / 1.2)

                if m >= 0:
                    moff = 128 * (m % 2)

                    def f_mk(e, pbx=pbx, colstart=colstart, moff=moff):
                        pv = pbx[:].rearrange("p (h q) -> p h q", h=2)[:, :, colstart:colstart + 128]
                        mv = mask_sb[:].rearrange("p (h q) -> p h q", h=2)[:, :, moff:moff + 128]
                        return e.tensor_tensor(pv, pv, mv, ALU.mult)
                    SS.add("dve", f_mk, reads=("mask_sb", f"pb{kt % 4}"),
                           writes=(f"pb{kt % 4}",), dbg=f"mask_{p}_{c}_{kt}",
                           cost=190)

                g2 = (p // 2) % 2
                q_ = p % 2
                for h2 in range(2):
                    def f_av(e, g2=g2, q_=q_, kt=kt, h2=h2, colstart=colstart, pbx=pbx):
                        return e.matmul(
                            ps_av[0:65, 512 * h2 + colstart:512 * (h2 + 1)],
                            vpg[g2][:, 260 * kt + 130 * q_ + 65 * h2:260 * kt + 130 * q_ + 65 * (h2 + 1)],
                            pbx[:, 512 * h2 + colstart:512 * (h2 + 1)],
                            start=(kt == 0), stop=False)
                    SS.add("pe", f_av, reads=(f"vpg{g2}_{kt // 2}", f"pb{kt % 4}"),
                           writes=("psav",), dbg=f"av_{p}_{c}_{kt}_{h2}",
                           cost=(512 - colstart) * 0.45 + 70)

                for _ in range(2):
                    if pending_jit:
                        o = pending_jit.pop(0)
                        SS.add(*o[0:2], reads=o[2], writes=o[3], dbg=o[4],
                               cost=o[5])

            emit_sc(0)
            for kt in range(1, nkt):
                emit_sc(kt)
                emit_tail(kt - 1)
            emit_tail(nkt - 1)

            for h2 in range(2):
                def f_cl(e, h2=h2):
                    return e.matmul(ps_av[0:65, 512 * h2:512 * (h2 + 1)],
                                    zrow[0:1, 0:65], zrow[0:1, 0:512],
                                    start=False, stop=True)
                SS.add("pe", f_cl, reads=("zrow",), writes=("psav",),
                       dbg=f"avcl_{p}_{c}_{h2}", cost=280)
            SS.add("dve", lambda e: e.tensor_copy(avs[0:96, :], ps_av[0:96, :]),
                   reads=("psav",), writes=("avs",), dbg=f"avev_{p}_{c}",
                   cost=1170)
            SS.add("dve", lambda e: e.transpose(d_t[0:32, :], avs[64:96, 0:1024]),
                   reads=("avs",), writes=("d_t",), dbg=f"dvt1_{p}_{c}",
                   cost=1200)

            def f_rc(e):
                iv = d_t[0:32, :].rearrange("p (a b) -> p a b", b=32)[:, :, 0:1]
                ov = d_ti[0:32, :].rearrange("p (a b) -> p a b", b=32)[:, :, 0:1]
                return e.reciprocal(ov, iv)
            SS.add("dve", f_rc, reads=("d_t",), writes=("d_ti",), dbg=f"drc_{p}_{c}",
                   cost=250)
            SS.add("dve", lambda e: e.transpose(d_t[0:32, :], d_ti[0:32, :]),
                   reads=("d_ti",), writes=("d_t",), dbg=f"dvt2_{p}_{c}",
                   cost=1200)
            SS.add("dve", lambda e: e.stream_shuffle(d_bc[0:32, :], d_t[0:32, :], [0] * 32),
                   reads=("d_t",), writes=("d_bc",), dbg=f"dsh1_{p}_{c}", cost=650)
            SS.add("dve", lambda e: e.stream_shuffle(d_bc[32:64, :], d_t[0:32, :], [0] * 32),
                   reads=("d_t",), writes=("d_bc",), dbg=f"dsh2_{p}_{c}", cost=650)
            for h2 in range(2):
                def f_oe(e, p=p, c=c, h2=h2):
                    return e.tensor_tensor(
                        o_all[64 * h2:64 * (h2 + 1), 2048 * p + 512 * c:2048 * p + 512 * (c + 1)],
                        avs[0:64, 512 * h2:512 * (h2 + 1)],
                        d_bc[:, 512 * h2:512 * (h2 + 1)], ALU.mult)
                SS.add("dve", f_oe, reads=("avs", "d_bc"), writes=("o_all",),
                       dbg=f"oev_{p}_{c}_{h2}", cost=460)
        if p < 7:
            for o in pending_jit:
                SS.add(*o[0:2], reads=o[2], writes=o[3], dbg=o[4], cost=o[5])
            pending_jit = jit_ops(p + 2) if p + 2 < 8 else []
        if p in (0, 1, 3):
            emit_jit_v({0: 1, 1: 2, 3: 3}[p])

    if debug:
        SS.dma("sp", lambda e: e.dma_start(dbg_t["ckvf"], ckvf[:]), reads=("ckvf",), sem="dbg", dbg="dbg1")
        SS.dma("sp", lambda e: e.dma_start(dbg_t["cqf"], cqf[:]), reads=("cqf",), sem="dbg", dbg="dbg2")
        SS.dma("sp", lambda e: e.dma_start(dbg_t["kpair"], kp[1][:]), reads=("kp1",), sem="dbg", dbg="dbg3")
        SS.dma("sp", lambda e: e.dma_start(dbg_t["vpair"], vpg[1][:]), reads=("vpg1",), sem="dbg", dbg="dbg4")
        SS.dma("sp", lambda e: e.dma_start(dbg_t["qpair"], qp[1][:]), reads=("qp1",), sem="dbg", dbg="dbg5")
        SS.dma("sp", lambda e: e.dma_start(dbg_t["oall"], o_all[:]), reads=("o_all",), sem="dbg", dbg="dbg6")

    SS.barrier()
    free("ps_s0", "ps_s1", "ps_av", "ps_j0", "ps_j1")
    free("kp0", "kp1", "vpg0", "vpg1", "qp0", "qp1", "pb0", "pb1", "pb2",
         "pb3", "d_t", "d_ti", "d_bc", "zrow", "avs")
    free("ckvf", "cqf", "stage0", "stage1", "stageg0", "stageg1")

    NPAR = 3
    xc = [sb(f"xc{i}", [128, D], F32) for i in range(NPAR)]
    attn_n = [sb(f"attn_n{i}", [128, D], F32) for i in range(NPAR)]
    x2 = [sb(f"x2_{i}", [128, D], F32) for i in range(NPAR)]
    xn2b = [sb(f"xn2b{i}", [128, D], BF16) for i in range(NPAR)]
    xn2f = [sb(f"xn2f{i}", [128, D], BF16) for i in range(NPAR)]
    texp = [sb(f"texp{i}", [128, 255], F32) for i in range(NPAR)]
    both = [sb(f"both{i}", [128, 510], F32) for i in range(NPAR)]
    trees2 = [[sb(f"treeA{i}", [128, 256], F32), sb(f"treeB{i}", [128, 256], F32)]
              for i in range(NPAR)]
    lpb = [sb(f"lpb{i}", [128, 256], BF16) for i in range(NPAR)]
    lpf = [sb(f"lpf{i}", [128, 256], BF16) for i in range(NPAR)]
    ffnn = [sb(f"ffnn{i}", [128, D], F32) for i in range(NPAR)]
    outt = [sb(f"outt{i}", [128, D], F32) for i in range(NPAR)]
    sqc = [sb(f"sqc{i}", [128, D], F32) for i in range(NPAR)]
    ps_c1 = [psum(f"ps_c1{i}", [128, 1024], F32) for i in range(NPAR)]
    ps_trb = [psum("ps_trb0", [128, 1024], BF16), psum("ps_trb1", [128, 1024], BF16)]

    for tau in (range(OWN) if 'C' in phases else []):
        par = tau % NPAR

        def fx(e, tau=tau, bx=xc[par]):
            return e.dma_start(bx[:], xre[:, 2 * tau, :])
        SS.dma("act", fx, reads=(f"xc_{par}",), writes=(f"xc_{par}",),
               sem=f"xc_{par}", dbg=f"ldxc_{tau}", cost=2500)
        xck = f"xc_{par}"

        def xcv(bx=xc[par]):
            return bx[:]

        pc1 = ps_c1[par]
        pc1k = f"psc1{par}"
        for p8 in range(8):
            for half in range(2):
                def f_mm(e, p8=p8, half=half, tau=tau, pc1=pc1):
                    return e.matmul(pc1[:, 512 * half:512 * (half + 1)],
                                    o_all[:, 2048 * p8 + 128 * tau:2048 * p8 + 128 * (tau + 1)],
                                    wot[:, D * p8 + 512 * half:D * p8 + 512 * (half + 1)],
                                    start=(p8 == 0), stop=(p8 == 7))
                SS.add("pe", f_mm, reads=("o_all", "wot"), writes=(pc1k,),
                       dbg=f"wo_{tau}_{p8}_{half}", cost=300)
        sca, bia, ka = ln_stats(f"attn_{tau}", f"attn_{par}", pc1k,
                                (lambda pc1=pc1: pc1[:]), D, sqc[par], f"sqc{par}")

        def f_apa(e, o=attn_n[par], pc1=pc1, sca=sca, bia=bia):
            return e.activation(o[:], pc1[:], AF.Identity, bias=bia, scale=sca)
        SS.add("act", f_apa, reads=(pc1k, ka + "r", ka + "nmr"),
               writes=(f"attn_n{par}",), dbg=f"attnap_{tau}", cost=1150)

        def f_x2(e, o=x2[par], a=attn_n[par], xcv=xcv):
            return e.tensor_tensor(o[:], a[:], xcv(), ALU.add)
        SS.add("dve", f_x2, reads=(f"attn_n{par}", xck), writes=(f"x2_{par}",),
               dbg=f"x2_{tau}", cost=1170)

        sc2, bi2, k2c = ln_stats(f"ln2_{tau}", f"ln2_{par}", f"x2_{par}",
                                 (lambda o=x2[par]: o[:]), D, sqc[par], f"sqc{par}")

        def f_ap2(e, o=xn2b[par], i=x2[par], sc2=sc2, bi2=bi2):
            return e.activation(o[:], i[:], AF.Identity, bias=bi2, scale=sc2)
        SS.add("act", f_ap2, reads=(f"x2_{par}", k2c + "r", k2c + "nmr"),
               writes=(f"xn2b{par}",), dbg=f"ln2ap_{tau}", cost=1150)

        ptr = ps_trb[tau % 2]
        ptrk = f"pstrb{tau % 2}"
        pgg = pc1
        pggk = pc1k
        trees = trees2[par]
        for cch in range(8):
            def f_t2(e, cch=cch, i=xn2b[par], ptr=ptr):
                return e.transpose(ptr[:, 128 * cch:128 * (cch + 1)],
                                   i[:, 128 * cch:128 * (cch + 1)], ident_b[:])
            SS.add("pe", f_t2, reads=(f"xn2b{par}", "ident_b"), writes=(ptrk,),
                   dbg=f"x2T_{tau}_{cch}", cost=320)
        SS.add("dve", (lambda o=xn2f[par], ptr=ptr: lambda e: e.tensor_copy(o[:], ptr[:]))(),
               reads=(ptrk,), writes=(f"xn2f{par}",), dbg=f"x2Te_{tau}", cost=700)

        for dc in range(8):
            def f_g(e, dc=dc, i=xn2f[par], pgg=pgg):
                return e.matmul(pgg[:, 0:255], i[:, 128 * dc:128 * (dc + 1)],
                                wgT[:, 255 * dc:255 * (dc + 1)],
                                start=(dc == 0), stop=(dc == 7))
            SS.add("pe", f_g, reads=(f"xn2f{par}", "wgT"), writes=(pggk,),
                   dbg=f"gate_{tau}_{dc}", cost=190)

        def f_te(e, o=texp[par], pgg=pgg):
            return e.activation(o[:], pgg[:, 0:255], AF.Exp, scale=-1.0)
        SS.add("act", f_te, reads=(pggk,), writes=(f"texp{par}",), dbg=f"texp_{tau}",
               cost=500)

        bview = both[par][:].rearrange("p (a b) -> p a b", b=2)

        def f_lg(e, o=bview, i=texp[par]):
            return e.activation(o[:, :, 1:2], i[:].rearrange("p (a b) -> p a b", b=1),
                                AF.Ln, bias=1.0)
        SS.add("act", f_lg, reads=(f"texp{par}",), writes=(f"both{par}",),
               dbg=f"lg_{tau}", cost=500)

        def f_gx(e, o=bview):
            return e.activation(o[:, :, 1:2], o[:, :, 1:2], AF.Exp, scale=-1.0)
        SS.add("act", f_gx, reads=(f"both{par}",), writes=(f"both{par}",),
               dbg=f"gx_{tau}", cost=500)

        def f_omg(e, o=bview, a=texp[par]):
            return e.tensor_tensor(o[:, :, 0:1],
                                   a[:].rearrange("p (a b) -> p a b", b=1),
                                   o[:, :, 1:2], ALU.mult)
        SS.add("dve", f_omg, reads=(f"texp{par}", f"both{par}"),
               writes=(f"both{par}",), dbg=f"omg_{tau}", cost=370)
        SS.add("dve", (lambda o=trees[0], i=both[par]:
                       lambda e: e.tensor_copy(o[:, 0:2], i[:, 0:2]))(),
               reads=(f"both{par}",), writes=(f"treeA{par}",), dbg=f"tr0_{tau}",
               cost=80)
        cur = 0
        for lv in range(1, 8):
            n = 1 << lv
            nxt = 1 - cur

            def f_lv(e, cur=cur, nxt=nxt, n=n, par=par, trees=trees):
                ov = trees[nxt][:, 0:2 * n].rearrange("p (a b) -> p a b", b=2)
                iv = trees[cur][:, 0:n].rearrange("p (a b) -> p a b", b=1)
                gv = both[par][:, 2 * (n - 1):2 * (2 * n - 1)].rearrange(
                    "p (a b) -> p a b", b=2)
                return e.tensor_tensor(ov, iv.broadcast_to((128, n, 2)), gv,
                                       ALU.mult)
            SS.add("dve", f_lv,
                   reads=((f"treeA{par}" if cur == 0 else f"treeB{par}"),
                          f"both{par}"),
                   writes=(f"treeB{par}" if nxt == 1 else f"treeA{par}",),
                   dbg=f"tr{lv}_{tau}", cost=120 + 2 * n)
            cur = nxt
        lpsrc = trees[cur]
        lpsrck = (f"treeA{par}" if cur == 0 else f"treeB{par}")
        SS.add("dve", (lambda o=lpb[par], i=lpsrc: lambda e: e.tensor_copy(o[:], i[:]))(),
               reads=(lpsrck,), writes=(f"lpb{par}",), dbg=f"lpb_{tau}", cost=370)
        for cch in range(2):
            def f_tl(e, cch=cch, i=lpb[par], ptr=ptr):
                return e.transpose(ptr[:, 128 * cch:128 * (cch + 1)],
                                   i[:, 128 * cch:128 * (cch + 1)], ident_b[:])
            SS.add("pe", f_tl, reads=(f"lpb{par}", "ident_b"), writes=(ptrk,),
                   dbg=f"lpT_{tau}_{cch}", cost=320)
        SS.add("dve", (lambda o=lpf[par], ptr=ptr: lambda e: e.tensor_copy(o[:], ptr[:, 0:256]))(),
               reads=(ptrk,), writes=(f"lpf{par}",), dbg=f"lpTe_{tau}", cost=370)
        for lc in range(2):
            for half in range(2):
                def f_of(e, lc=lc, half=half, i=lpf[par], pc1=pc1):
                    return e.matmul(pc1[:, 512 * half:512 * (half + 1)],
                                    i[:, 128 * lc:128 * (lc + 1)],
                                    w2[:, D * lc + 512 * half:D * lc + 512 * (half + 1)],
                                    start=(lc == 0), stop=False)
                SS.add("pe", f_of, reads=(f"lpf{par}", "w2"), writes=(pc1k,),
                       dbg=f"of_{tau}_{lc}_{half}", cost=300)
        for half in range(2):
            def f_bias(e, half=half, pc1=pc1):
                return e.matmul(pc1[:, 512 * half:512 * (half + 1)], ones_b[:],
                                woutb_b[:, 512 * half:512 * (half + 1)],
                                start=False, stop=True)
            SS.add("pe", f_bias, reads=("ones_b", "woutb_b"), writes=(pc1k,),
                   dbg=f"ofb_{tau}_{half}", cost=300)
        scf, bif, kf = ln_stats(f"ffn_{tau}", f"ffn_{par}", pc1k,
                                (lambda pc1=pc1: pc1[:]), D, sqc[par], f"sqc{par}")

        def f_apf(e, o=ffnn[par], pc1=pc1, scf=scf, bif=bif):
            return e.activation(o[:], pc1[:], AF.Identity, bias=bif, scale=scf)
        SS.add("act", f_apf, reads=(pc1k, kf + "r", kf + "nmr"),
               writes=(f"ffnn{par}",), dbg=f"ffnap_{tau}", cost=1150)

        def f_out(e, o=outt[par], a=ffnn[par], b_=x2[par]):
            return e.tensor_tensor(o[:], a[:], b_[:], ALU.add)
        SS.add("dve", f_out, reads=(f"ffnn{par}", f"x2_{par}"),
               writes=(f"outt{par}",), dbg=f"out_{tau}", cost=1170)

        def f_st(e, tau=tau, o=outt[par]):
            return e.dma_start(outd[128 * tau:128 * (tau + 1), :], o[:])
        SS.dma("sp", f_st, reads=(f"outt{par}",), writes=(),
               sem=f"outt{par}", dbg=f"st_{tau}", cost=2500)

    import os
    if os.environ.get("NOREORDER", "") != "1":
        SS.reorder()
    SS.resolve()

    sems = {}
    semguards = []
    for e in ENGINES:
        g = nc.semaphore(f"prog_{e}")
        sems[e] = g.__enter__()
        semguards.append(g)
    for q in sorted(SS.final_dma):
        g = nc.semaphore(q)
        sems[q] = g.__enter__()
        semguards.append(g)

    blk_g = nc.Block()
    blk = blk_g.__enter__()

    def run_engine(ename):
        def body(eng):
            for op in SS.ops:
                if op.get("barrier"):
                    for t in op["bwaits"].get(ename, []):
                        if t[0] == "eng":
                            v = SS.waitval(t[1], t[2])
                            if v > 0:
                                eng.wait_ge(sems[t[1]], v)
                        else:
                            eng.wait_ge(sems[t[1]], t[2])
                    continue
                if op["engine"] != ename:
                    continue
                for t in op["waits"]:
                    if t[0] == "eng":
                        v = SS.waitval(t[1], t[2])
                        if v > 0:
                            eng.wait_ge(sems[t[1]], v)
                    else:
                        eng.wait_ge(sems[t[1]], t[2])
                ins = op["fn"](eng)
                if op["dma"]:
                    ins.then_inc(sems[op["dma"]], 16)
                elif op["pos"] in SS.wait_targets[ename]:
                    ins.then_inc(sems[ename], 1)
            if ename == "sp":
                for q, c in SS.final_dma.items():
                    if c > 0:
                        eng.wait_ge(sems[q], c)
        return body

    blk.tensor(run_engine("pe"))
    blk.scalar(run_engine("act"))
    blk.vector(run_engine("dve"))
    blk.sync(run_engine("sp"))
    blk_g.__exit__(None, None, None)
    for n in reversed(list(alive)):
        alive.pop(n).__exit__(None, None, None)
    for g in semguards:
        g.__exit__(None, None, None)
    return nc


def make_in_maps(inputs, debug=False):
    x = np.asarray(inputs["L0_attn_x"], dtype=np.float32)
    tri = np.triu(np.ones((128, 128), np.float32))
    import ml_dtypes

    def bf(a):
        return np.ascontiguousarray(a).astype(ml_dtypes.bfloat16)

    common = {
        "w_dq": _pack_chunks(np.asarray(inputs["L0_attn_W_dq"], np.float32), 8),
        "w_uq": _pack_chunks(np.asarray(inputs["L0_attn_W_uq"], np.float32), 2),
        "w_dkv": _pack_chunks(np.asarray(inputs["L0_attn_W_dkv"], np.float32), 8),
        "w_ukv": _pack_chunks(np.asarray(inputs["L0_attn_W_ukv"], np.float32), 2),
        "w_ot": _pack_chunks(np.ascontiguousarray(np.asarray(inputs["L0_attn_W_o"], np.float32).T), 8),
        "w_q": _pack_chunks(np.asarray(inputs["L0_ffn_Wq"], np.float32), 8),
        "w_rt": _pack_chunks(np.ascontiguousarray(np.asarray(inputs["L0_ffn_routers"], np.float32).T), 8),
        "w_lvt": _pack_chunks(np.ascontiguousarray(np.asarray(inputs["L0_ffn_leaf_values"], np.float32).T), 8),
        "w_outt": _pack_chunks(np.ascontiguousarray(np.asarray(inputs["L0_ffn_Wout_w"], np.float32).T), 8),
        "w_outb": np.asarray(inputs["L0_ffn_Wout_b"], np.float32).reshape(1, D),
        "identb": np.eye(128, dtype=np.float32),
        "onesb": np.ones((1, 128), np.float32),
    }
    qg = np.asarray(inputs["L0_attn_q_ln_g"], np.float32).reshape(2, 128).T
    qb = np.asarray(inputs["L0_attn_q_ln_b"], np.float32).reshape(2, 128).T
    kg = np.asarray(inputs["L0_attn_kv_ln_g"], np.float32).reshape(2, 128).T
    kb = np.asarray(inputs["L0_attn_kv_ln_b"], np.float32).reshape(2, 128).T
    common["vecs"] = np.ascontiguousarray(
        np.concatenate([qg, qb, kg, kb], axis=1))

    maps = []
    for core in range(8):
        b, par = core // 2, core % 2
        xs = x[b].reshape(NT, 128, D)
        if par == 1:
            perm = np.arange(NT).reshape(-1, 2)[:, ::-1].reshape(-1)
            xs = xs[perm]
        odd = np.ones((128, 128), np.float32) if par == 1 else np.zeros((128, 128), np.float32)
        m4 = np.ascontiguousarray(np.concatenate([tri, odd, tri, odd], axis=1))
        mm = dict(common)
        mm["xseq"] = np.ascontiguousarray(xs.reshape(S, D))
        mm["mask4"] = m4
        maps.append(mm)
    return maps


def kernel(**inputs):
    nc = build_program(debug=False)
    maps = make_in_maps(inputs)
    res = run_bass_kernel_spmd(nc, maps, core_ids=list(range(8)))
    out = np.empty((4, S, D), dtype=np.float32)
    for core in range(8):
        b, par = core // 2, core % 2
        o = res.results[core]["out"].reshape(OWN, 128, D)
        out.reshape(4, NT, 128, D)[b, par::2] = o
    return out
